# version 07
# baseline (speedup 1.0000x reference)
import contextlib
import math

import numpy as np
import ml_dtypes

import concourse.bass as bass
import concourse.mybir as mybir
from concourse.bass_utils import run_bass_kernel_spmd

F32 = mybir.dt.float32
BF16 = mybir.dt.bfloat16
I32 = mybir.dt.int32
AF = mybir.ActivationFunctionType
ALU = mybir.AluOpType
AX = mybir.AxisListType

N_CORES = 8
T_CORE = 2048
N_TILES = 16
D = 1024
HD = 64
SEQ = 8192
NE = 8
H_E = 768
H_S = 1536
NS_E = 6
NS_S = 12
N_UNITS = NE * NS_E + NS_S
KC = 8

TWO_PI = 2.0 * math.pi
INV_2PI = 1.0 / TWO_PI
C1 = float(np.float32(TWO_PI))
C2 = float(np.float64(TWO_PI) - np.float64(np.float32(TWO_PI)))
PI = math.pi
EPS_LN = 1e-5
DEN_EPS = 1e-6

ARENA_COLS = 98304

TAPS = ()


class Arena:
    def __init__(self, tensor, cols):
        self.t = tensor
        self.free_list = [(0, cols)]
        self.live = {}
        self.hwm = 0

    def alloc(self, name, cols, dtype=BF16, rows=128):
        unit = 1 if dtype == BF16 else 2
        need = (cols * unit + 15) & ~15
        for idx, (off, sz) in enumerate(self.free_list):
            if sz >= need:
                self.free_list[idx] = (off + need, sz - need)
                if self.free_list[idx][1] == 0:
                    del self.free_list[idx]
                self.live[name] = (off, need)
                self.hwm = max(self.hwm, sum(v[1] for v in self.live.values()))
                ap = self.t[0:rows, off:off + need]
                if dtype != BF16:
                    ap = ap.bitcast(dtype)
                return ap
        raise MemoryError(f"arena: no space for {name} ({need}); "
                          f"free={self.free_list}")

    def free(self, *names):
        for name in names:
            off, sz = self.live.pop(name)
            self.free_list.append((off, sz))
        self.free_list.sort()
        merged = []
        for o, s in self.free_list:
            if merged and merged[-1][0] + merged[-1][1] == o:
                merged[-1][1] += s
            else:
                merged.append([o, s])
        self.free_list = [tuple(x) for x in merged]


class P:
    def __init__(self, nc, ctx):
        self.nc = nc
        self.q = {"tensor": [], "vector": [], "scalar": [], "gpsimd": [],
                  "sync": []}
        self.sem = {}
        self.ctx = ctx

    def mksem(self, *names):
        for n in names:
            h = self.ctx.enter_context(self.nc.semaphore(n))
            self.sem[n] = [h, 0]

    def c(self, name):
        return self.sem[name][1]

    def bump(self, name, n):
        self.sem[name][1] += n
        return self.sem[name][0], self.sem[name][1]

    def wait(self, eng, name, val):
        h = self.sem[name][0]
        if val > 0:
            self.q[eng].append(lambda e: e.wait_ge(h, val))

    def op(self, eng, fn, inc=None, inc_n=1, inc2=None):
        if inc is None:
            self.q[eng].append(fn)
        else:
            h, _ = self.bump(inc, inc_n)
            if inc2 is None:
                self.q[eng].append(lambda e: fn(e).then_inc(h, inc_n))
            else:
                h2, _ = self.bump(inc2, 1)
                self.q[eng].append(
                    lambda e: fn(e).then_inc(h, inc_n).then_inc(h2, 1))

    def emit(self, block):
        for name in ("sync", "gpsimd", "tensor", "vector", "scalar"):
            fns = self.q[name]
            getattr(block, name)(lambda e, fns=fns: [f(e) for f in fns])


def build_program(taps=(), phases=9.0):
    nc = bass.Bass("TRN2", target_bir_lowering=False, debug=False)
    ctx = contextlib.ExitStack()

    di = lambda n, s, dt=F32: nc.dram_tensor(n, s, dt, kind="ExternalInput").ap()
    do = lambda n, s, dt=F32: nc.dram_tensor(n, s, dt, kind="ExternalOutput").ap()

    x_in = di("x", [T_CORE, D])
    xT_in = di("xT", [D, T_CORE])
    w_qkv = di("w_qkv", [D, 3 * D])
    w_out_in = di("w_out", [D, D])
    gate_wT = di("gate_wT", [D, NE])
    w1s = di("w1s", [NE, NS_E, 8, 128, 256])
    ws1s = di("ws1s", [NS_S, 8, 128, 256])
    w2T_in = di("w2T", [NE, H_E, D])
    ws2T_in = di("ws2T", [H_S, D])
    consts_in = di("consts", [8, T_CORE])
    signcol = di("signcol", [128, 1])
    pperm = di("pperm", [128, 128], BF16)
    sel16 = di("sel16", [16, 8 * 128], BF16)
    ident = di("ident", [128, 128], BF16)
    esel = di("esel", [8, 8 * 128], BF16)

    out_t = do("out", [T_CORE, D])

    xn32_h = nc.dram_tensor("xn32_h", [T_CORE, D], F32, kind="Internal")
    xnb_h = nc.dram_tensor("xnb_h", [T_CORE, D], BF16, kind="Internal")
    act_h = nc.dram_tensor("act_h", [N_UNITS, 128, T_CORE], BF16,
                           kind="Internal")
    cc_in = nc.dram_tensor("cc_in", [128, 1040], F32, kind="Internal")
    cc_out = nc.dram_tensor("cc_out", [128, 1040], F32, kind="Internal")

    tap_aps = {}
    for t in taps:
        shape, dt = {
            "q_feat": ([128, 8 * T_CORE], BF16), "kv_full": ([128, 1040], F32),
            "attn": ([128, 8 * T_CORE], BF16), "xn": ([T_CORE, D], F32),
            "scores": ([128, 128], F32), "wrow": ([16, T_CORE], BF16),
            "cosk": ([128, N_TILES * HD], F32),
            "sink": ([128, N_TILES * HD], F32),
            "cosq": ([128, T_CORE], F32), "sinq": ([128, T_CORE], F32),
            "den": ([16, T_CORE], F32), "recip": ([16, T_CORE], BF16),
            "acth": ([4, 128, T_CORE], BF16),
            "xn2": ([T_CORE, D], F32),
            "res0": ([1, 128, D], BF16),
        }[t]
        tap_aps[t] = do("tap_" + t, shape, dt)

    with ctx:
        arena_t = ctx.enter_context(
            nc.sbuf_tensor("arena", [128, ARENA_COLS], BF16))
        A = Arena(arena_t, ARENA_COLS)
        p = P(nc, ctx)
        p.mksem("s_in", "s_gx", "s_gw", "s_gw2", "s_str", "s_w2l", "s_xt",
                "s_xnl", "s_act", "s_st", "s_cc", "s_coll", "s_out",
                "s_mm", "s_vv", "s_ss", "s_tab", "s_ev", "s_qf",
                "s_kfree", "s_vfree", "s_kvin", "s_kvmm", "s_varg", "s_exp",
                "s_qbf", "s_qfree", "s_rotfree", "s_bev", "s_vmul",
                "s_res", "s_sg", "s_p1", "s_actst", "s_zdone", "s_vout",
                "s_gatev", "s_wr")

        scr_a = A.alloc("scr_a", T_CORE, F32)
        scr_b = A.alloc("scr_b", T_CORE, F32)
        sg_scr = A.alloc("sg_scr", T_CORE, F32)
        scr_i = sg_scr.bitcast(I32)
        ln_st = A.alloc("ln_st", 16, F32)
        ones_r = A.alloc("ones_r", 128, rows=1)
        ones_c = A.alloc("ones_c", 8)
        ident_sb = A.alloc("ident_sb", 128)
        sel16_sb = A.alloc("sel16_sb", 8 * 128, rows=16)
        x_fm = A.alloc("x_fm", 8 * T_CORE)
        wqkv_sb = A.alloc("wqkv", 8 * 3072)
        q_feat = A.alloc("q_feat", 8 * T_CORE)
        rows_c = A.alloc("rows_c", T_CORE, F32)
        ivf_r = A.alloc("ivf_r", 512, F32, rows=1)
        sign_sb = A.alloc("sign_sb", 8, F32)
        pperm_sb = A.alloc("pperm_sb", 128)
        cos_k = A.alloc("cos_k", N_TILES * HD, F32)
        sink_s = A.alloc("sink_s", N_TILES * HD, F32)
        cos_q = A.alloc("cos_q", T_CORE, F32)
        sinq_s = A.alloc("sinq_s", T_CORE, F32)
        kfeat_t = A.alloc("kfeat", 1024)
        v_bf = A.alloc("v_bf", 1024)
        q_bf = A.alloc("q_bf", 1024)
        kv_sb = A.alloc("kv_sb", 1040, F32)
        kv_full = A.alloc("kv_full", 1040, F32)
        kv_bf = A.alloc("kv_bf", 1024)
        ksum16 = A.alloc("ksum16", 8 * 16)

        trow_r = rows_c[0:1, :]
        ifhi_r = ivf_r[0:1, 0:HD]
        iflo_r = ivf_r[0:1, HD:2 * HD]
        if128h_r = ivf_r[0:1, 128:256]
        if128l_r = ivf_r[0:1, 256:384]

        pA = ctx.enter_context(nc.psum_tensor("pA", [128, 2048], F32))
        pB = ctx.enter_context(nc.psum_tensor("pB", [128, 1024], F32))
        pCb = ctx.enter_context(nc.psum_tensor("pCb", [128, 2048], BF16))
        pC = pCb[:].bitcast(F32)

        def ld(eng, dst, src, sem):
            p.op(eng, lambda e, d=dst, s=src: e.dma_start(out=d, in_=s),
                 inc=sem, inc_n=16)

        def mm(out, lhsT, rhs, st, sp):
            return lambda e: e.matmul(out, lhsT, rhs, start=st, stop=sp,
                                      skip_group_check=True)

        def finalize():
            tap_srcs = {
                "q_feat": lambda: q_feat, "attn": lambda: attn_fm,
                "kv_full": lambda: kv_full, "cosk": lambda: cos_k,
                "sink": lambda: sink_s, "cosq": lambda: cos_q,
                "sinq": lambda: sinq_s, "den": lambda: den_r,
                "recip": lambda: recip_r, "scores": lambda: scores_sb,
                "wrow": lambda: wrow,
            }
            L = locals()
            for s in list(p.sem):
                cnt = p.c(s)
                if cnt and s not in ("s_st", "s_out"):
                    p.wait("sync", s, cnt)
            for t in taps:
                if t == "res0":
                    ld("sync", tap_aps[t][0], dbg_t, "s_st")
                elif t == "acth":
                    for ii, u in enumerate((0, 5, 30, 48)):
                        ld("sync", tap_aps[t][ii], act_h.ap()[u], "s_st")
                elif t == "xn":
                    ld("sync", tap_aps[t], xn32_h.ap(), "s_st")
                elif t in tap_srcs:
                    try:
                        srcv = tap_srcs[t]()
                    except NameError:
                        continue
                    if t == "wrow":
                        ld("sync", tap_aps[t][0:8, :], srcv[0:8, :], "s_st")
                    else:
                        ld("sync", tap_aps[t], srcv, "s_st")
            if p.c("s_st"):
                p.wait("sync", "s_st", p.c("s_st"))
            if p.c("s_out"):
                p.wait("sync", "s_out", p.c("s_out"))
            block = ctx.enter_context(nc.Block())
            p.emit(block)
            return nc

        ld("sync", rows_c[0:1, :], consts_in[0:1, :], "s_in")
        ld("sync", ivf_r[0:1, 0:512], consts_in[1:2, 0:512], "s_in")
        ld("sync", sign_sb[:, 0:1], signcol, "s_in")
        ld("sync", pperm_sb, pperm, "s_in")
        ld("sync", sel16_sb, sel16, "s_in")
        ld("sync", ident_sb, ident, "s_in")
        n_in = p.c("s_in")

        for c in range(8):
            ld("gpsimd", x_fm[:, c * T_CORE:(c + 1) * T_CORE],
               xT_in[c * 128:(c + 1) * 128, :], "s_gx")
        gx_all = p.c("s_gx")
        for c in range(8):
            ld("gpsimd", wqkv_sb[:, c * 3072 + 1024:c * 3072 + 3072],
               w_qkv[c * 128:(c + 1) * 128, 1024:3072], "s_gw")
        gw_kv = p.c("s_gw")
        for c in range(8):
            ld("gpsimd", wqkv_sb[:, c * 3072:c * 3072 + 1024],
               w_qkv[c * 128:(c + 1) * 128, 0:1024], "s_gw")
        gw_all = p.c("s_gw")

        if phases <= 0.2:
            return finalize()

        p.wait("tensor", "s_in", n_in)
        for i in range(N_TILES):
            sl = trow_r[:, i * 128:(i + 1) * 128]
            p.op("tensor", mm(pB[:, i * HD:(i + 1) * HD], sl, ifhi_r,
                              i % 8 == 0, False))
            p.op("tensor", mm(pB[:, i * HD:(i + 1) * HD], sl, iflo_r,
                              False, True),
                 inc="s_mm" if i == N_TILES - 1 else None)
        mm_ktab = p.c("s_mm")
        for q4 in range(4):
            sl = trow_r[:, q4 * 512:(q4 + 1) * 512]
            p.op("tensor", mm(pA[:, q4 * 512:(q4 + 1) * 512], if128h_r, sl,
                              True, False))
            p.op("tensor", mm(pA[:, q4 * 512:(q4 + 1) * 512], if128l_r, sl,
                              False, True), inc="s_mm" if q4 == 3 else None)
        mm_qtab = p.c("s_mm")
        if phases <= 0.4:
            return finalize()

        p.wait("vector", "s_in", n_in)
        p.op("vector", lambda e: e.memset(ones_r, 1.0))
        p.op("vector", lambda e: e.memset(ones_c[:, 0:1], 1.0))
        p.op("vector", lambda e: e.memset(kv_sb, 0.0))
        p.op("vector", lambda e: e.memset(ksum16, 0.0))

        def range_reduce(v, src, n, r_dst, rc_dst):
            kf = scr_a[:, 0:n]
            v.tensor_single_scalar(kf, src, INV_2PI, ALU.mult)
            v.tensor_copy(scr_i[:, 0:n], kf)
            v.tensor_copy(kf, scr_i[:, 0:n])
            v.scalar_tensor_tensor(r_dst, kf, -C1, src, ALU.mult, ALU.add)
            v.scalar_tensor_tensor(r_dst, kf, -C2, r_dst, ALU.mult, ALU.add)
            v.tensor_single_scalar(rc_dst, r_dst, PI / 2, ALU.add)
            v.tensor_single_scalar(kf, rc_dst, PI, ALU.is_gt)
            return v.scalar_tensor_tensor(rc_dst, kf, -TWO_PI, rc_dst,
                                          ALU.mult, ALU.add)

        nk = N_TILES * HD
        p.wait("vector", "s_mm", mm_ktab)
        p.op("vector", lambda e: range_reduce(e, pB[:, 0:nk], nk,
                                              scr_b[:, 0:nk], pB[:, 0:nk]),
             inc="s_vv")
        vv_kt = p.c("s_vv")
        p.wait("vector", "s_mm", mm_qtab)
        p.op("vector", lambda e: range_reduce(e, pA[:, 0:T_CORE], T_CORE,
                                              scr_b[:, 0:T_CORE],
                                              pA[:, 0:T_CORE]), inc="s_vv")
        vv_qt = p.c("s_vv")
        if phases <= 0.6:
            return finalize()

        p.wait("scalar", "s_vv", vv_kt)
        p.op("scalar", lambda e: e.activation(sink_s, scr_b[:, 0:nk], AF.Sin))
        p.op("scalar", lambda e: e.activation(cos_k, pB[:, 0:nk], AF.Sin),
             inc="s_ss")
        ss_kt = p.c("s_ss")
        p.wait("scalar", "s_vv", vv_qt)
        p.op("scalar", lambda e: e.activation(sinq_s, scr_b[:, 0:T_CORE],
                                              AF.Sin))
        p.op("scalar", lambda e: e.activation(cos_q, pA[:, 0:T_CORE], AF.Sin),
             inc="s_ss")
        ss_qt = p.c("s_ss")
        if phases <= 0.8:
            return finalize()

        p.wait("vector", "s_ss", ss_kt)
        sk4 = sink_s.rearrange("p (i a b) -> p i a b", i=N_TILES, a=2)
        p.op("vector", lambda e: e.tensor_single_scalar(
            sk4[:, :, 0, :], sk4[:, :, 0, :], -1.0, ALU.mult), inc="s_tab")
        tab_k = p.c("s_tab")
        p.wait("vector", "s_ss", ss_qt)
        p.op("vector", lambda e: e.tensor_single_scalar(cos_q, cos_q, 0.125,
                                                        ALU.mult))
        p.op("vector", lambda e: e.tensor_scalar(sinq_s, sinq_s,
                                                 sign_sb[:, 0:1], 0.125,
                                                 ALU.mult, ALU.mult),
             inc="s_tab")
        tab_q = p.c("s_tab")
        if phases <= 1:
            return finalize()

        p.wait("tensor", "s_gx", gx_all)
        p.wait("tensor", "s_gw", gw_kv)
        p.wait("tensor", "s_tab", tab_k)
        mm_1a0 = p.c("s_mm")
        for i in range(N_TILES):
            if i > 0:
                p.wait("tensor", "s_kfree", i)
            for kc in range(KC):
                lhs = x_fm[:, kc * T_CORE + i * 128:kc * T_CORE + (i + 1) * 128]
                for hf in range(2):
                    p.op("tensor", mm(pB[:, hf * 512:(hf + 1) * 512], lhs,
                                      wqkv_sb[:, kc * 3072 + 1024 + hf * 512:
                                              kc * 3072 + 1024 + (hf + 1) * 512],
                                      kc == 0, kc == KC - 1),
                         inc="s_mm" if kc == KC - 1 and hf == 1 else None)
            if i > 0:
                p.wait("tensor", "s_vfree", i)
            for kc in range(KC):
                lhs = x_fm[:, kc * T_CORE + i * 128:kc * T_CORE + (i + 1) * 128]
                for hf in range(2):
                    p.op("tensor", mm(pC[:, hf * 512:(hf + 1) * 512], lhs,
                                      wqkv_sb[:, kc * 3072 + 2048 + hf * 512:
                                              kc * 3072 + 2048 + (hf + 1) * 512],
                                      kc == 0, kc == KC - 1),
                         inc="s_mm" if kc == KC - 1 and hf == 1 else None)
            p.wait("tensor", "s_kvin", i + 1)
            for c in range(8):
                lhs = kfeat_t[:, c * 128:(c + 1) * 128]
                p.op("tensor", mm(pA[:, c * 128:(c + 1) * 128], lhs,
                                  v_bf[:, c * 128:(c + 1) * 128],
                                  i == 0 and c in (0, 4), i == N_TILES - 1))
                p.op("tensor", mm(pA[:, 1024 + c * 16:1024 + c * 16 + 1], lhs,
                                  ones_c[:, 0:1], i == 0 and c == 0,
                                  i == N_TILES - 1),
                     inc=("s_kvmm" if c == 7 and i < N_TILES - 1 else
                          "s_mm" if c == 7 and i == N_TILES - 1 else None))
        mm_kv_done = p.c("s_mm")

        kp4 = pB[:].rearrange("p (h a b) -> p h a b", h=16, a=2)
        kp3 = pB[:].rearrange("p (h d) -> p h d", h=16)
        for i in range(N_TILES):
            cs_b = cos_k[:, i * HD:(i + 1) * HD].unsqueeze(1).broadcast_to(
                [128, 16, HD])
            sn = sink_s[:, i * HD:(i + 1) * HD].rearrange(
                "p (a b) -> p a b", a=2)
            sn0 = sn[:, 0, :].unsqueeze(1).broadcast_to([128, 16, 32])
            sn1 = sn[:, 1, :].unsqueeze(1).broadcast_to([128, 16, 32])
            tc = scr_a[:, 0:1024]
            ts = scr_b[:, 0:1024]
            tc3 = tc.rearrange("p (h d) -> p h d", h=16)
            ts4 = ts.rearrange("p (h a b) -> p h a b", h=16, a=2)
            p.wait("vector", "s_mm", mm_1a0 + 2 * i + 1)
            p.op("vector", lambda e, a=tc3, c=cs_b:
                 e.tensor_tensor(a, kp3, c, ALU.mult))
            p.op("vector", lambda e, a=ts4, c=sn0:
                 e.tensor_tensor(a[:, :, 0, :], kp4[:, :, 1, :], c, ALU.mult))
            p.op("vector", lambda e, a=ts4, c=sn1:
                 e.tensor_tensor(a[:, :, 1, :], kp4[:, :, 0, :], c, ALU.mult),
                 inc="s_kfree")
            p.op("vector", lambda e, a=tc, b=ts:
                 e.tensor_tensor(a, a, b, ALU.add))
            p.op("vector", lambda e, a=ts, b=tc:
                 e.tensor_single_scalar(a, b, 0.0, ALU.min), inc="s_varg")
            p.wait("vector", "s_mm", mm_1a0 + 2 * i + 2)
            if i > 0:
                p.wait("vector", "s_kvmm", i)
            p.op("vector", lambda e: e.tensor_copy(v_bf, pC[:]),
                 inc="s_vfree")
            p.wait("vector", "s_exp", i + 1)
            p.op("vector", lambda e, a=tc: e.scalar_tensor_tensor(
                kfeat_t, a, 0.0, sg_scr[:, 0:1024], ALU.max, ALU.add),
                inc="s_kvin")
        for i in range(N_TILES):
            p.wait("scalar", "s_varg", i + 1)
            p.op("scalar", lambda e: e.activation(sg_scr[:, 0:1024],
                                                  scr_b[:, 0:1024], AF.Exp),
                 inc="s_exp")

        p.wait("vector", "s_mm", mm_kv_done)

        def kv_evac(e):
            for c in range(8):
                e.tensor_copy(kv_sb[0:64, c * 128:c * 128 + 64],
                              pA[0:64, c * 128:c * 128 + 64])
                e.tensor_copy(kv_sb[64:128, c * 128 + 64:c * 128 + 128],
                              pA[64:128, c * 128 + 64:c * 128 + 128])
                e.tensor_copy(kv_sb[:, 1024 + 2 * c:1024 + 2 * c + 1],
                              pA[:, 1024 + c * 16:1024 + c * 16 + 1])
            return e.tensor_copy(kv_sb[0:1, 0:1], kv_sb[0:1, 0:1])
        p.op("vector", kv_evac, inc="s_ev")
        ev_kv = p.c("s_ev")

        p.wait("sync", "s_ev", ev_kv)
        ld("sync", cc_in.ap(), kv_sb, "s_cc")
        cc1 = p.c("s_cc")
        p.wait("gpsimd", "s_cc", cc1)
        p.op("gpsimd", lambda e: e.collective_compute(
            "AllReduce", ALU.add, replica_groups=[[0, 1, 2, 3], [4, 5, 6, 7]],
            ins=[cc_in.ap()], outs=[cc_out.ap()]), inc="s_coll")
        p.wait("sync", "s_coll", 1)
        ld("sync", kv_full, cc_out.ap(), "s_cc")
        cc2 = p.c("s_cc")

        def ksum_build(e):
            for c in range(8):
                e.tensor_copy(ksum16[0:64, c * 16 + 2 * c:c * 16 + 2 * c + 1],
                              kv_full[0:64, 1024 + 2 * c:1024 + 2 * c + 1])
                e.tensor_copy(
                    ksum16[64:128, c * 16 + 2 * c + 1:c * 16 + 2 * c + 2],
                    kv_full[64:128, 1024 + 2 * c:1024 + 2 * c + 1])
            return e.tensor_copy(ksum16[0:1, 0:1], ksum16[0:1, 0:1])
        if phases <= 2:
            return finalize()

        p.wait("tensor", "s_gw", gw_all)
        p.wait("tensor", "s_tab", tab_q)
        p.wait("tensor", "s_ev", ev_kv)
        p.wait("tensor", "s_kfree", N_TILES)
        p.wait("tensor", "s_vfree", N_TILES)
        mm_q0 = p.c("s_mm")
        for idx in range(16):
            m, hf = idx // 2, idx % 2
            pq = pB if hf == 0 else pC
            if idx >= 2:
                p.wait("tensor", "s_qfree", idx - 1)
            for kc in range(KC):
                lhs = wqkv_sb[:, kc * 3072 + m * 128:kc * 3072 + (m + 1) * 128]
                for pp in range(2):
                    p.op("tensor", mm(
                        pq[:, pp * 512:(pp + 1) * 512], lhs,
                        x_fm[:, kc * T_CORE + hf * 1024 + pp * 512:
                             kc * T_CORE + hf * 1024 + (pp + 1) * 512],
                        kc == 0, kc == KC - 1),
                        inc="s_mm" if kc == KC - 1 and pp == 1 else None)
            p.wait("tensor", "s_qbf", idx + 1)
            if idx >= 2:
                p.wait("tensor", "s_rotfree", idx - 1)
            pr = pA[:, (idx % 2) * 1024:(idx % 2) * 1024 + 1024]
            for pp in range(2):
                p.op("tensor", mm(pr[:, pp * 512:(pp + 1) * 512], pperm_sb,
                                  q_bf[:, pp * 512:(pp + 1) * 512],
                                  True, True),
                     inc="s_mm" if pp == 1 else None)

        for idx in range(16):
            m, hf = idx // 2, idx % 2
            pq = pB if hf == 0 else pC
            tsl = slice(hf * 1024, (hf + 1) * 1024)
            pr = pA[:, (idx % 2) * 1024:(idx % 2) * 1024 + 1024]
            p.wait("vector", "s_mm", mm_q0 + 2 * idx + 1)
            p.op("vector", lambda e, a=pq: e.tensor_copy(q_bf, a[:]),
                 inc="s_qbf")
            p.op("vector", lambda e, a=pq, t=tsl: e.tensor_tensor(
                scr_a[:, 0:1024], a[:], cos_q[:, t], ALU.mult), inc="s_qfree")
            p.wait("vector", "s_mm", mm_q0 + 2 * idx + 2)
            p.op("vector", lambda e, a=pr, t=tsl: e.tensor_tensor(
                scr_b[:, 0:1024], a, sinq_s[:, t], ALU.mult), inc="s_rotfree")
            p.op("vector", lambda e: e.tensor_tensor(
                scr_a[:, 0:1024], scr_a[:, 0:1024], scr_b[:, 0:1024],
                ALU.add))
            p.op("vector", lambda e: e.tensor_single_scalar(
                scr_b[:, 0:1024], scr_a[:, 0:1024], 0.0, ALU.min),
                inc="s_varg")
            p.wait("vector", "s_exp", 16 + idx + 1)
            p.op("vector", lambda e, m=m, hf=hf: e.scalar_tensor_tensor(
                q_feat[:, m * T_CORE + hf * 1024:m * T_CORE + (hf + 1) * 1024],
                scr_a[:, 0:1024], 0.0, sg_scr[:, 1024:2048], ALU.max,
                ALU.add), inc="s_qf")
        for idx in range(16):
            p.wait("scalar", "s_varg", 16 + idx + 1)
            p.op("scalar", lambda e: e.activation(sg_scr[:, 1024:2048],
                                                  scr_b[:, 0:1024], AF.Exp),
                 inc="s_exp")
        p.wait("vector", "s_cc", cc2)
        p.op("vector", lambda e: e.tensor_copy(kv_bf, kv_full[:, 0:1024]))
        p.op("vector", ksum_build, inc="s_ev")
        ev_kvbf = p.c("s_ev")
        qf_all = p.c("s_qf")
        mm_1b_done = p.c("s_mm")
        if phases <= 3:
            return finalize()

        A.free("wqkv", "cos_k", "sink_s", "rows_c", "sign_sb", "ivf_r")
        attn_fm = A.alloc("attn_fm", 8 * T_CORE)
        wout_sb = A.alloc("wout_sb", 8 * 1024)
        gate_sb = A.alloc("gate_sb", 64)
        den_r = A.alloc("den_r", T_CORE, F32, rows=16)
        recip_r = A.alloc("recip_r", T_CORE, rows=16)
        p.wait("gpsimd", "s_mm", mm_1b_done)
        for c in range(8):
            ld("gpsimd", wout_sb[:, c * 1024:(c + 1) * 1024],
               w_out_in[c * 128:(c + 1) * 128, :], "s_gw2")
        for c in range(8):
            ld("gpsimd", gate_sb[:, c * 8:(c + 1) * 8],
               gate_wT[c * 128:(c + 1) * 128, :], "s_gw2")
        gw2_all = p.c("s_gw2")

        p.wait("tensor", "s_ev", ev_kvbf)
        p.wait("tensor", "s_qf", qf_all)
        p.wait("tensor", "s_rotfree", 16)
        for pp in range(4):
            for c in range(8):
                p.op("tensor", mm(
                    pA[0:16, pp * 512:(pp + 1) * 512],
                    ksum16[:, c * 16:(c + 1) * 16],
                    q_feat[:, c * T_CORE + pp * 512:c * T_CORE + (pp + 1) * 512],
                    c == 0, c == 7),
                    inc="s_mm" if pp == 3 and c == 7 else None)
        mm_den = p.c("s_mm")
        p.wait("vector", "s_mm", mm_den)
        p.op("vector", lambda e: e.tensor_single_scalar(
            den_r, pA[0:16, 0:T_CORE], DEN_EPS, ALU.max), inc="s_vv")
        vv_den = p.c("s_vv")
        p.wait("scalar", "s_vv", vv_den)
        p.op("scalar", lambda e: e.activation(den_r, den_r, AF.Ln))
        p.op("scalar", lambda e: e.activation(recip_r, den_r, AF.Exp,
                                              scale=-1.0), inc="s_ss")
        ss_recip = p.c("s_ss")

        p.wait("tensor", "s_ss", ss_recip)
        mm_p2 = p.c("s_mm")
        bev_p2 = p.c("s_bev")
        vmul_p2 = p.c("s_vmul")
        for c in range(8):
            for pp in range(4):
                idx = c * 4 + pp
                if idx >= 4:
                    p.wait("tensor", "s_vmul", vmul_p2 + idx - 3)
                p.op("tensor", mm(
                    pA[:, pp * 512:(pp + 1) * 512],
                    kv_bf[:, c * 128:(c + 1) * 128],
                    q_feat[:, c * T_CORE + pp * 512:c * T_CORE + (pp + 1) * 512],
                    True, True))
                if idx >= 2:
                    p.wait("tensor", "s_bev", bev_p2 + idx - 1)
                p.op("tensor", mm(
                    pB[:, (pp % 2) * 512:(pp % 2) * 512 + 512],
                    sel16_sb[:, c * 128:(c + 1) * 128],
                    recip_r[:, pp * 512:(pp + 1) * 512], True, True),
                    inc="s_mm")
        wb2 = scr_b
        for c in range(8):
            for pp in range(4):
                idx = c * 4 + pp
                sl = slice((idx % 2) * 512, (idx % 2) * 512 + 512)
                p.wait("scalar", "s_mm", mm_p2 + idx + 1)
                p.op("scalar", lambda e, s=sl, pp=pp:
                     e.copy(wb2[:, s], pB[:, (pp % 2) * 512:
                                          (pp % 2) * 512 + 512]),
                     inc="s_bev")
                p.wait("vector", "s_bev", bev_p2 + idx + 1)
                p.op("vector", lambda e, c=c, pp=pp, s=sl: e.tensor_tensor(
                    attn_fm[:, c * T_CORE + pp * 512:
                            c * T_CORE + (pp + 1) * 512],
                    pA[:, pp * 512:(pp + 1) * 512], wb2[:, s], ALU.mult),
                    inc="s_vmul")
        vmul_all = p.c("s_vmul")
        if phases <= 4:
            return finalize()

        A.free("q_bf", "kv_sb", "kv_full", "cos_q", "sinq_s", "pperm_sb")
        dbg_t = A.alloc("dbg_t", D, BF16) if "res0" in taps else None
        x_tile = A.alloc("x_tile", 2 * D, F32)
        res_t = A.alloc("res_t", 2 * D, F32)
        xnb_t = A.alloc("xnb_t", 2 * D)

        p.wait("tensor", "s_vmul", vmul_all)
        p.wait("tensor", "s_bev", bev_p2 + 32)
        mm_p3 = p.c("s_mm")
        for i in range(N_TILES):
            pz = pB if i % 2 == 0 else pC
            if i >= 2:
                p.wait("tensor", "s_res", i - 1)
            for kc in range(KC):
                lhs = attn_fm[:, kc * T_CORE + i * 128:
                              kc * T_CORE + (i + 1) * 128]
                for hf in range(2):
                    p.op("tensor", mm(pz[:, hf * 512:(hf + 1) * 512], lhs,
                                      wout_sb[:, kc * 1024 + hf * 512:
                                              kc * 1024 + (hf + 1) * 512],
                                      kc == 0, kc == KC - 1),
                         inc="s_mm" if kc == KC - 1 and hf == 1 else None)

        import os as _os
        _NO_ACC = _os.environ.get("K_NO_ACCUM") == "1"

        def ln_stats(e, src, o, scr):
            e.reduce_sum(ln_st[:, o:o + 1], src, AX.X)
            if _NO_ACC:
                e.tensor_tensor(scr[:, 0:D], src, src, ALU.mult)
                e.reduce_sum(ln_st[:, o + 1:o + 2], scr[:, 0:D], AX.X)
            else:
                e.scalar_tensor_tensor(scr[:, 0:D], src, 1.0, src, ALU.mult,
                                       ALU.mult,
                                       accum_out=ln_st[:, o + 1:o + 2])
            e.tensor_single_scalar(ln_st[:, o + 2:o + 3],
                                   ln_st[:, o:o + 1], 1.0 / D, ALU.mult)
            e.tensor_single_scalar(ln_st[:, o + 1:o + 2],
                                   ln_st[:, o + 1:o + 2], 1.0 / D, ALU.mult)
            e.tensor_tensor(ln_st[:, o + 3:o + 4], ln_st[:, o + 2:o + 3],
                            ln_st[:, o + 2:o + 3], ALU.mult)
            e.tensor_tensor(ln_st[:, o + 1:o + 2], ln_st[:, o + 1:o + 2],
                            ln_st[:, o + 3:o + 4], ALU.subtract)
            return e.tensor_single_scalar(ln_st[:, o + 1:o + 2],
                                          ln_st[:, o + 1:o + 2], EPS_LN,
                                          ALU.add)

        vv_p3_0 = p.c("s_vv")

        def p3_store(i):
            p.wait("sync", "s_vv", vv_p3_0 + i + 1)
            ld("sync", xn32_h.ap()[i * 128:(i + 1) * 128, :],
               res_t[:, (i % 2) * D:(i % 2 + 1) * D], "s_st")
            ld("sync", xnb_h.ap()[i * 128:(i + 1) * 128, :],
               xnb_t[:, (i % 2) * D:(i % 2 + 1) * D], "s_st")

        st_p3_base = p.c("s_st")
        for i in range(N_TILES):
            if i >= 2:
                p.wait("sync", "s_res", i - 1)
            ld("sync", x_tile[:, (i % 2) * D:(i % 2 + 1) * D],
               x_in[i * 128:(i + 1) * 128, :], "s_xt")
            if i >= 2:
                p3_store(i - 2)
        p3_store(N_TILES - 2)
        p3_store(N_TILES - 1)

        for i in range(N_TILES):
            pz = pB if i % 2 == 0 else pC
            xs = x_tile[:, (i % 2) * D:(i % 2 + 1) * D]
            rs = res_t[:, (i % 2) * D:(i % 2 + 1) * D]
            p.wait("vector", "s_mm", mm_p3 + i + 1)
            p.wait("vector", "s_xt", 16 * (i + 1))
            if i >= 2:
                p.wait("vector", "s_st", 32 * (i - 1))
            p.op("vector", lambda e, a=rs, b=xs, c=pz:
                 e.tensor_tensor(a, b, c[:], ALU.add), inc="s_res")
            if dbg_t is not None and i == 0:
                p.op("vector", lambda e, a=rs: e.tensor_copy(dbg_t, a))
            p.op("vector", lambda e, a=rs: ln_stats(e, a, 0, scr_b),
                 inc="s_varg")
            p.wait("vector", "s_exp", 32 + i + 1)
            p.op("vector", lambda e, a=rs: e.tensor_scalar(
                a, a, ln_st[:, 2:3], ln_st[:, 4:5], ALU.subtract, ALU.mult))
            p.op("vector", lambda e, a=rs, i=i: e.tensor_copy(
                xnb_t[:, (i % 2) * D:(i % 2 + 1) * D], a), inc="s_vv")
        xnl0 = p.c("s_xnl")

        def emit_tr_group(t0, t1):
            p.wait("scalar", "s_st", st_p3_base + 32 * t1)
            r0, r1 = t0 * 128, t1 * 128
            for c in range(8):
                p.op("scalar", lambda e, c=c, r0=r0, r1=r1:
                     e.dma_start_transpose(
                         x_fm[:, c * T_CORE + r0:c * T_CORE + r1],
                         xnb_h.ap()[r0:r1, c * 128:(c + 1) * 128]),
                     inc="s_xnl", inc_n=16)

        for i in range(N_TILES):
            p.wait("scalar", "s_varg", 32 + i + 1)
            p.op("scalar", lambda e: e.activation(ln_st[:, 4:5],
                                                  ln_st[:, 1:2], AF.Ln))
            p.op("scalar", lambda e: e.activation(ln_st[:, 4:5],
                                                  ln_st[:, 4:5], AF.Exp,
                                                  scale=-0.5), inc="s_exp")
            if i == 6:
                emit_tr_group(0, 4)
            elif i == 10:
                emit_tr_group(4, 8)
            elif i == 14:
                emit_tr_group(8, 12)
        emit_tr_group(12, 14)
        emit_tr_group(14, 15)
        emit_tr_group(15, 16)
        xnl_all = p.c("s_xnl")
        st_xn = p.c("s_st")
        if "xn2" in taps:
            p.wait("sync", "s_st", st_xn)
            ld("sync", tap_aps["xn2"], xn32_h.ap(), "s_st")
            st_xn = p.c("s_st")
        if phases <= 4.9:
            return finalize()

        xn_fm = x_fm
        if phases <= 5:
            return finalize()

        A.free("den_r", "recip_r", "kfeat", "v_bf")
        scores_sb = A.alloc("scores", 128, F32)
        esel_sb = A.alloc("esel_sb", 8 * 128, rows=8)
        gst = A.alloc("gst", 64, F32)
        w_tmb = A.alloc("w_tmb", 128)
        wrow = A.alloc("wrow", T_CORE, rows=16)

        ld("sync", esel_sb[0:8, :], esel, "s_in")
        n_in2 = p.c("s_in")
        p.wait("tensor", "s_gw2", gw2_all)
        p.wait("tensor", "s_res", 16)

        def xnl_thr(i):
            g = 1 if i < 4 else 2 if i < 8 else 3 if i < 12 else \
                4 if i < 14 else 5 if i < 15 else 6
            return xnl0 + 128 * g
        xnl_prev = [None]
        for i in range(N_TILES):
            if xnl_thr(i) != xnl_prev[0]:
                p.wait("tensor", "s_xnl", xnl_thr(i))
                xnl_prev[0] = xnl_thr(i)
            for kc in range(KC):
                lhs = xn_fm[:, kc * T_CORE + i * 128:kc * T_CORE + (i + 1) * 128]
                p.op("tensor", mm(pA[:, i * 8:(i + 1) * 8], lhs,
                                  gate_sb[:, kc * 8:(kc + 1) * 8],
                                  kc == 0, kc == KC - 1),
                     inc="s_mm" if i == N_TILES - 1 and kc == KC - 1 else None)
        mm_gate = p.c("s_mm")
        p.wait("vector", "s_mm", mm_gate)

        def gate_v1(e):
            s3 = scores_sb.rearrange("p (i g) -> p i g", g=8)
            e.tensor_copy(scores_sb, pA[:, 0:128])
            e.tensor_reduce(gst[:, 0:16], s3, AX.X, ALU.max)
            g16 = gst[:, 0:16].unsqueeze(2).broadcast_to([128, 16, 8])
            return e.tensor_tensor(s3, s3, g16, ALU.subtract)
        p.op("vector", gate_v1, inc="s_vv")
        vv_g1 = p.c("s_vv")
        p.wait("scalar", "s_vv", vv_g1)
        p.op("scalar", lambda e: e.activation(scores_sb, scores_sb, AF.Exp),
             inc="s_ss")
        ss_g = p.c("s_ss")
        p.wait("vector", "s_ss", ss_g)

        def gate_v2(e):
            s3 = scores_sb.rearrange("p (i g) -> p i g", g=8)
            a3 = scr_a[:, 0:128].rearrange("p (i g) -> p i g", g=8)
            b3 = scr_b[:, 0:128].rearrange("p (i g) -> p i g", g=8)
            e.tensor_reduce(gst[:, 0:16], s3, AX.X, ALU.add)
            e.tensor_copy(gst[:, 48:64], gst[:, 0:16])
            e.reciprocal(gst[:, 16:32], gst[:, 0:16])
            e.tensor_copy(gst[:, 48:64], gst[:, 0:16])
            g16 = gst[:, 16:32].unsqueeze(2).broadcast_to([128, 16, 8])
            e.tensor_tensor(s3, s3, g16, ALU.mult)
            e.tensor_reduce(gst[:, 0:16], s3, AX.X, ALU.max)
            m16 = gst[:, 0:16].unsqueeze(2).broadcast_to([128, 16, 8])
            e.tensor_tensor(a3, s3, m16, ALU.is_ge)
            e.scalar_tensor_tensor(scr_b[:, 0:128], scr_a[:, 0:128], -2.0,
                                   scores_sb, ALU.mult, ALU.add)
            e.tensor_reduce(gst[:, 32:48], b3, AX.X, ALU.max)
            m2 = gst[:, 32:48].unsqueeze(2).broadcast_to([128, 16, 8])
            e.tensor_tensor(b3, s3, m2, ALU.is_ge)
            e.tensor_tensor(scores_sb, scores_sb, scr_b[:, 0:128], ALU.mult)
            return e.tensor_copy(w_tmb, scores_sb)
        p.op("vector", gate_v2, inc="s_gatev")
        p.wait("tensor", "s_gatev", 1)
        for i in range(N_TILES):
            p.op("tensor", lambda e, i=i: e.matmul(
                pCb[0:8, i * 128:(i + 1) * 128],
                w_tmb[:, i * 8:(i + 1) * 8], ident_sb, start=True, stop=True,
                is_transpose=True, skip_group_check=True), inc="s_mm")
        mm_wr = p.c("s_mm")
        p.wait("vector", "s_mm", mm_wr)
        p.op("vector", lambda e: e.tensor_copy(wrow[0:8, :],
                                               pCb[0:8, 0:T_CORE]),
             inc="s_wr")

        if phases <= 6:
            return finalize()

        units = [("e", e_, s_) for e_ in range(NE) for s_ in range(NS_E)] + \
                [("s", 0, s_) for s_ in range(NS_S)]
        A.free("x_tile", "xnb_t", "kv_bf", "ksum16", "scores", "gst",
               "w_tmb", "res_t", "q_feat", "attn_fm", "wout_sb", "gate_sb",
               "sel16_sb", "ident_sb", "scr_a", "scr_b")
        stripe_w = A.alloc("stripe_w", 2 * 8 * 256)
        wb_bf = A.alloc("wb_bf", T_CORE)
        p1scr = A.alloc("p1scr", T_CORE)
        act_st = A.alloc("act_st", 2 * T_CORE)
        w2slots = {}
        n_w2_early = 48
        for u in range(48):
            w2slots[u] = A.alloc(f"w2_{u}", 1024)
        try:
            for u in range(48, N_UNITS):
                w2slots[u] = A.alloc(f"w2_{u}", 1024)
            n_w2_early = N_UNITS
        except MemoryError:
            pass

        def w2src(u):
            if u < NE * NS_E:
                e_, s_ = u // NS_E, u % NS_E
                return w2T_in[e_, s_ * 128:(s_ + 1) * 128, :]
            s_ = u - NE * NS_E
            return ws2T_in[s_ * 128:(s_ + 1) * 128, :]

        p.wait("gpsimd", "s_wr", 1)
        p.wait("gpsimd", "s_st", st_xn)

        def emit_stripe_load(uix):
            kind, e_, s_ = units[uix]
            buf = uix % 2
            if uix >= 2:
                p.wait("gpsimd", "s_p1", uix - 1)
            src = w1s[e_, s_] if kind == "e" else ws1s[s_]
            dst = stripe_w[:, buf * 2048:(buf + 1) * 2048]
            p.op("gpsimd", lambda e, s=src, d=dst: e.dma_start(
                out=d.rearrange("p (j c) -> p j c", j=8),
                in_=s.transpose([1, 0, 2])), inc="s_str", inc_n=16)

        emit_stripe_load(0)
        emit_stripe_load(1)
        n_w2 = 0
        for uix in range(N_UNITS):
            if uix + 2 < N_UNITS:
                emit_stripe_load(uix + 2)
            if n_w2 < n_w2_early:
                ld("gpsimd", w2slots[n_w2], w2src(n_w2), "s_w2l")
                n_w2 += 1
        while n_w2 < n_w2_early:
            ld("gpsimd", w2slots[n_w2], w2src(n_w2), "s_w2l")
            n_w2 += 1

        p.wait("tensor", "s_wr", 1)
        st_xn_base = st_xn
        for uix, (kind, e_, s_) in enumerate(units):
            buf = uix % 2
            if kind == "e" and s_ == 0:
                if uix > 0:
                    p.wait("tensor", "s_sg", uix)
                p.wait("tensor", "s_in", n_in2)
                for pp in range(4):
                    p.op("tensor", mm(pA[:, pp * 512:(pp + 1) * 512],
                                      esel_sb[0:8, e_ * 128:(e_ + 1) * 128],
                                      wrow[0:8, pp * 512:(pp + 1) * 512],
                                      True, True),
                         inc="s_mm" if pp == 3 else None)
                mmv = p.c("s_mm")
                p.wait("scalar", "s_mm", mmv)
                if uix > 0:
                    p.wait("scalar", "s_actst", uix)
                p.op("scalar", lambda e: e.copy(wb_bf, pA[:, 0:T_CORE]),
                     inc="s_bev")
                p.wait("tensor", "s_bev", p.c("s_bev"))
            p.wait("tensor", "s_str", (uix + 1) * 16)
            if uix > 0 and not (kind == "e" and s_ == 0):
                p.wait("tensor", "s_sg", uix)
            for kc in range(KC):
                lhs = stripe_w[:, buf * 2048 + kc * 256 + 128:
                               buf * 2048 + kc * 256 + 256]
                for pp in range(4):
                    p.op("tensor", mm(pA[:, pp * 512:(pp + 1) * 512], lhs,
                                      xn_fm[:, kc * T_CORE + pp * 512:
                                            kc * T_CORE + (pp + 1) * 512],
                                      kc == 0, kc == KC - 1),
                         inc="s_mm" if kc == KC - 1 and pp == 3 else None)
            mm_g = p.c("s_mm")
            if uix > 0:
                p.wait("tensor", "s_p1", uix)
            for kc in range(KC):
                lhs = stripe_w[:, buf * 2048 + kc * 256:
                               buf * 2048 + kc * 256 + 128]
                for hf in range(2):
                    pq = pB if hf == 0 else pC
                    for pp in range(2):
                        p.op("tensor", mm(
                            pq[:, pp * 512:(pp + 1) * 512], lhs,
                            xn_fm[:, kc * T_CORE + hf * 1024 + pp * 512:
                                  kc * T_CORE + hf * 1024 + (pp + 1) * 512],
                            kc == 0, kc == KC - 1),
                            inc="s_mm" if kc == KC - 1 and hf == 1 and pp == 1
                            else None)
            mm_y = p.c("s_mm")
            p.wait("scalar", "s_mm", mm_g)
            if uix > 0:
                p.wait("scalar", "s_p1", uix)
            p.op("scalar", lambda e: e.activation(sg_scr, pA[:, 0:T_CORE],
                                                  AF.Silu), inc="s_sg")
            a0 = (uix % 2) * T_CORE
            p.wait("vector", "s_mm", mm_y)
            p.wait("vector", "s_sg", uix + 1)
            if uix >= 2:
                p.wait("vector", "s_st", st_xn_base + 16 * (uix - 1))
            if kind == "e":
                p.op("vector", lambda e: e.tensor_tensor(
                    p1scr[:, 0:1024], pB[:], sg_scr[:, 0:1024], ALU.mult))
                p.op("vector", lambda e: e.tensor_tensor(
                    p1scr[:, 1024:2048], pC[:], sg_scr[:, 1024:2048],
                    ALU.mult), inc="s_p1")
                p.op("vector", lambda e, a0=a0: e.tensor_tensor(
                    act_st[:, a0:a0 + T_CORE], p1scr, wb_bf, ALU.mult),
                    inc="s_actst")
            else:
                p.op("vector", lambda e, a0=a0: e.tensor_tensor(
                    act_st[:, a0:a0 + 1024], pB[:], sg_scr[:, 0:1024],
                    ALU.mult))
                p.op("vector", lambda e, a0=a0: e.tensor_tensor(
                    act_st[:, a0 + 1024:a0 + 2048], pC[:],
                    sg_scr[:, 1024:2048], ALU.mult), inc="s_p1")
                p.op("vector", lambda e, a0=a0: e.tensor_copy(
                    act_st[0:1, a0:a0 + 1], act_st[0:1, a0:a0 + 1]),
                    inc="s_actst")
            p.wait("sync", "s_actst", p.c("s_actst"))
            ld("sync", act_h.ap()[uix, :, :], act_st[:, a0:a0 + T_CORE],
               "s_st")
        p1_all = p.c("s_p1")
        sg_all = p.c("s_sg")
        st_act = p.c("s_st")
        if phases <= 7:
            return finalize()

        A.free("x_fm", "stripe_w", "wb_bf", "p1scr", "act_st", "wrow",
               "sg_scr", "esel_sb")
        for u in range(n_w2_early, N_UNITS):
            w2slots[u] = A.alloc(f"w2_{u}", 1024)
        xn4b = A.alloc("xn4b", 2 * D, F32)
        act4b = A.alloc("act4b", 2 * 15 * 256)
        res4 = A.alloc("res4", 2 * D, F32)
        scr_b4 = A.alloc("scr_b4", D, F32)

        if n_w2_early < N_UNITS:
            p.wait("gpsimd", "s_p1", p1_all)
            p.wait("gpsimd", "s_st", st_act)
        for u in range(n_w2_early, N_UNITS):
            ld("gpsimd", w2slots[u], w2src(u), "s_w2l")
        w2l_all = p.c("s_w2l")

        NBLK, NGRP = 8, 4
        act0 = p.c("s_act")
        if phases <= 7.5:
            p.wait("sync", "s_p1", p1_all)
            p.wait("sync", "s_st", st_act)
            for gi in range(2):
                for part in range(3):
                    srcx = act_h.ap()[gi * 15 + part * 5:gi * 15 + part * 5 + 5,
                                      :, 0:256]
                    d0 = (gi % 2) * 3840 + part * 5 * 256
                    p.op("sync", lambda e, s=srcx, d0=d0: e.dma_start(
                        out=act4b[:, d0:d0 + 1280].rearrange(
                            "p (j c) -> p j c", j=5),
                        in_=s.transpose([1, 0, 2])), inc="s_act", inc_n=16)
            ld("sync", xn4b[:, 0:D], xn32_h.ap()[0:128, :], "s_xnl")
            return finalize()

        def out_store(i):
            p.wait("sync", "s_vout", i + 1)
            ld("sync", out_t[i * 128:(i + 1) * 128, :],
               res4[:, (i % 2) * D:(i % 2 + 1) * D], "s_out")

        p.wait("sync", "s_p1", p1_all)
        p.wait("sync", "s_st", st_act)
        for b in range(NBLK):
            for g in range(NGRP):
                gi = b * NGRP + g
                if gi >= 2:
                    p.wait("sync", "s_zdone", gi - 1)
                u0 = g * 15
                for part in range(3):
                    src = act_h.ap()[u0 + part * 5:u0 + part * 5 + 5, :,
                                     b * 256:(b + 1) * 256]
                    d0 = (gi % 2) * 3840 + part * 5 * 256
                    dst = act4b[:, d0:d0 + 5 * 256]
                    p.op("sync", lambda e, s=src, d=dst: e.dma_start(
                        out=d.rearrange("p (j c) -> p j c", j=5),
                        in_=s.transpose([1, 0, 2])), inc="s_act", inc_n=16)
            for tt in range(2):
                i = b * 2 + tt
                if i >= 2:
                    p.wait("sync", "s_vout", i - 1)
                ld("sync", xn4b[:, tt * D:(tt + 1) * D],
                   xn32_h.ap()[i * 128:(i + 1) * 128, :], "s_xnl")
            if b >= 1:
                out_store(2 * b - 2)
                out_store(2 * b - 1)

        p.wait("tensor", "s_w2l", w2l_all)
        p.wait("tensor", "s_p1", p1_all)
        p.wait("tensor", "s_sg", sg_all)
        mm_4b0 = p.c("s_mm")
        for b in range(NBLK):
            if b > 0:
                p.wait("tensor", "s_res", 16 + 2 * b - 1)
                p.wait("tensor", "s_res", 16 + 2 * b)
            for g in range(NGRP):
                gi = b * NGRP + g
                p.wait("tensor", "s_act", act0 + 48 * (gi + 1))
                for tt in range(2):
                    pz = pB if tt == 0 else pC
                    for j in range(15):
                        u = g * 15 + j
                        lhs = act4b[:, (gi % 2) * 3840 + j * 256 + tt * 128:
                                    (gi % 2) * 3840 + j * 256 + (tt + 1) * 128]
                        for hf in range(2):
                            last = j == 14 and hf == 1
                            inc = None
                            if last and tt == 1:
                                inc = "s_zdone"
                            elif last and tt == 0 and g == 3:
                                inc = "s_mm"
                            p.op("tensor", mm(
                                pz[:, hf * 512:(hf + 1) * 512], lhs,
                                w2slots[u][:, hf * 512:(hf + 1) * 512],
                                g == 0 and j == 0, u == N_UNITS - 1), inc=inc)

        for b in range(NBLK):
            for tt in range(2):
                i = b * 2 + tt
                pz = pB if tt == 0 else pC
                rs = res4[:, tt * D:(tt + 1) * D]
                if tt == 0:
                    p.wait("vector", "s_mm", mm_4b0 + b + 1)
                else:
                    p.wait("vector", "s_zdone", 4 * (b + 1))
                p.wait("vector", "s_xnl", xnl_all + 16 * (i + 1))
                if i >= 2:
                    p.wait("vector", "s_out", 16 * (i - 1))
                p.op("vector", lambda e, a=rs, tt=tt, c=pz:
                     e.tensor_tensor(a, xn4b[:, tt * D:(tt + 1) * D], c[:],
                                     ALU.add), inc="s_res")
                p.op("vector", lambda e, a=rs: ln_stats(e, a, 8, scr_b4),
                     inc="s_varg")
                p.wait("vector", "s_exp", 48 + i + 1)
                p.op("vector", lambda e, a=rs: e.tensor_scalar(
                    a, a, ln_st[:, 10:11], ln_st[:, 12:13], ALU.subtract,
                    ALU.mult), inc="s_vout")
        for i in range(16):
            p.wait("scalar", "s_varg", 48 + i + 1)
            p.op("scalar", lambda e: e.activation(ln_st[:, 12:13],
                                                  ln_st[:, 9:10], AF.Ln))
            p.op("scalar", lambda e: e.activation(ln_st[:, 12:13],
                                                  ln_st[:, 12:13], AF.Exp,
                                                  scale=-0.5), inc="s_exp")
        out_store(14)
        out_store(15)
        p.wait("sync", "s_out", p.c("s_out"))

        return finalize()
    return nc



def _host_constants():
    if64 = 1.0 / (500000.0 ** (np.arange(0, 64, 2, dtype=np.float64) / 64.0))
    hi = if64.astype(np.float32)
    lo = (if64 - hi.astype(np.float64)).astype(np.float32)
    signcol = np.where((np.arange(128) % 64) < 32, -1.0, 1.0
                       ).astype(np.float32).reshape(128, 1)
    Pm = np.zeros((128, 128), np.float32)
    for pp in range(128):
        base = 64 * (pp // 64)
        Pm[pp, base + ((pp % 64) + 32) % 64] = 1.0
    pperm = np.ascontiguousarray(Pm.T).astype(ml_dtypes.bfloat16)
    sel = np.zeros((16, 8 * 128), np.float32)
    for c in range(8):
        for m in range(128):
            sel[2 * c + (1 if m >= 64 else 0), c * 128 + m] = 1.0
    sel16 = sel.astype(ml_dtypes.bfloat16)
    ident = np.eye(128, dtype=ml_dtypes.bfloat16)
    esel = np.zeros((8, 8 * 128), np.float32)
    for e in range(8):
        esel[e, e * 128:(e + 1) * 128] = 1.0
    esel = esel.astype(ml_dtypes.bfloat16)
    return hi, lo, signcol, pperm, sel16, ident, esel


def make_in_maps(inputs):
    x = np.ascontiguousarray(np.asarray(inputs["L0_attn_x"], np.float32)
                             ).reshape(2 * SEQ, D)
    hi, lo, signcol, pperm, sel16, ident, esel = _host_constants()
    w_qkv = np.ascontiguousarray(np.asarray(inputs["L0_attn_W_qkv"],
                                            np.float32))
    w_out = np.ascontiguousarray(np.asarray(inputs["L0_attn_W_out"],
                                            np.float32))
    gate_wT = np.ascontiguousarray(
        np.asarray(inputs["L0_ffn_gate_w"], np.float32).T)
    W1T = np.ascontiguousarray(
        np.asarray(inputs["L0_ffn_W1"], np.float32).transpose(0, 2, 1))
    w1s = np.empty((NE, NS_E, D, 256), np.float32)
    for m in range(NS_E):
        w1s[:, m, :, 0:128] = W1T[:, :, 128 * m:128 * (m + 1)]
        w1s[:, m, :, 128:256] = W1T[:, :, H_E + 128 * m:H_E + 128 * (m + 1)]
    Ws1T = np.ascontiguousarray(np.asarray(inputs["L0_ffn_Ws1"],
                                           np.float32).T)
    ws1s = np.empty((NS_S, D, 256), np.float32)
    for m in range(NS_S):
        ws1s[m, :, 0:128] = Ws1T[:, 128 * m:128 * (m + 1)]
        ws1s[m, :, 128:256] = Ws1T[:, H_S + 128 * m:H_S + 128 * (m + 1)]
    w2T = np.ascontiguousarray(
        np.asarray(inputs["L0_ffn_W2"], np.float32).transpose(0, 2, 1))
    ws2T = np.ascontiguousarray(np.asarray(inputs["L0_ffn_Ws2"],
                                           np.float32).T)

    shared = dict(w_qkv=w_qkv, w_out=w_out, gate_wT=gate_wT,
                  w1s=w1s.reshape(NE, NS_E, 8, 128, 256),
                  ws1s=ws1s.reshape(NS_S, 8, 128, 256),
                  w2T=w2T, ws2T=ws2T, signcol=signcol,
                  pperm=pperm, sel16=sel16, ident=ident, esel=esel)
    in_maps = []
    for c in range(N_CORES):
        xs = np.ascontiguousarray(x[c * T_CORE:(c + 1) * T_CORE])
        t0 = (c % 4) * T_CORE
        consts = np.zeros((8, T_CORE), np.float32)
        consts[0] = t0 + np.arange(T_CORE, dtype=np.float32)
        consts[1, 0:HD] = np.concatenate([hi, hi])
        consts[1, HD:2 * HD] = np.concatenate([lo, lo])
        consts[1, 128:256] = np.concatenate([hi, hi, hi, hi])
        consts[1, 256:384] = np.concatenate([lo, lo, lo, lo])
        m = dict(x=xs, xT=np.ascontiguousarray(xs.T), consts=consts)
        m.update(shared)
        in_maps.append(m)
    return in_maps


def kernel(**inputs):
    in_maps = make_in_maps(inputs)
    nc = build_program(taps=TAPS)
    res = run_bass_kernel_spmd(nc, in_maps, core_ids=list(range(N_CORES)))
    outs = [r["out"] for r in res.results]
    return np.concatenate(outs, axis=0).reshape(2, SEQ, D)


# version 08
# speedup vs baseline: 1.0227x; 1.0227x over previous
import contextlib
import math

import numpy as np
import ml_dtypes

import concourse.bass as bass
import concourse.mybir as mybir
from concourse.bass_utils import run_bass_kernel_spmd

F32 = mybir.dt.float32
BF16 = mybir.dt.bfloat16
I32 = mybir.dt.int32
AF = mybir.ActivationFunctionType
ALU = mybir.AluOpType
AX = mybir.AxisListType

N_CORES = 8
T_CORE = 2048
N_TILES = 16
D = 1024
HD = 64
SEQ = 8192
NE = 8
H_E = 768
H_S = 1536
NS_E = 6
NS_S = 12
N_UNITS = NE * NS_E + NS_S
KC = 8

TWO_PI = 2.0 * math.pi
INV_2PI = 1.0 / TWO_PI
C1 = float(np.float32(TWO_PI))
C2 = float(np.float64(TWO_PI) - np.float64(np.float32(TWO_PI)))
PI = math.pi
EPS_LN = 1e-5
DEN_EPS = 1e-6

ARENA_COLS = 98304

TAPS = ()


class Arena:
    def __init__(self, tensor, cols):
        self.t = tensor
        self.free_list = [(0, cols)]
        self.live = {}
        self.hwm = 0

    def alloc(self, name, cols, dtype=BF16, rows=128):
        unit = 1 if dtype == BF16 else 2
        need = (cols * unit + 15) & ~15
        for idx, (off, sz) in enumerate(self.free_list):
            if sz >= need:
                self.free_list[idx] = (off + need, sz - need)
                if self.free_list[idx][1] == 0:
                    del self.free_list[idx]
                self.live[name] = (off, need)
                self.hwm = max(self.hwm, sum(v[1] for v in self.live.values()))
                ap = self.t[0:rows, off:off + need]
                if dtype != BF16:
                    ap = ap.bitcast(dtype)
                return ap
        raise MemoryError(f"arena: no space for {name} ({need}); "
                          f"free={self.free_list}")

    def free(self, *names):
        for name in names:
            off, sz = self.live.pop(name)
            self.free_list.append((off, sz))
        self.free_list.sort()
        merged = []
        for o, s in self.free_list:
            if merged and merged[-1][0] + merged[-1][1] == o:
                merged[-1][1] += s
            else:
                merged.append([o, s])
        self.free_list = [tuple(x) for x in merged]


class P:
    def __init__(self, nc, ctx):
        self.nc = nc
        self.q = {"tensor": [], "vector": [], "scalar": [], "gpsimd": [],
                  "sync": []}
        self.sem = {}
        self.ctx = ctx

    def mksem(self, *names):
        for n in names:
            h = self.ctx.enter_context(self.nc.semaphore(n))
            self.sem[n] = [h, 0]

    def c(self, name):
        return self.sem[name][1]

    def bump(self, name, n):
        self.sem[name][1] += n
        return self.sem[name][0], self.sem[name][1]

    def wait(self, eng, name, val):
        h = self.sem[name][0]
        if val > 0:
            self.q[eng].append(lambda e: e.wait_ge(h, val))

    def op(self, eng, fn, inc=None, inc_n=1, inc2=None):
        if inc is None:
            self.q[eng].append(fn)
        else:
            h, _ = self.bump(inc, inc_n)
            if inc2 is None:
                self.q[eng].append(lambda e: fn(e).then_inc(h, inc_n))
            else:
                h2, _ = self.bump(inc2, 1)
                self.q[eng].append(
                    lambda e: fn(e).then_inc(h, inc_n).then_inc(h2, 1))

    def emit(self, block):
        for name in ("sync", "gpsimd", "tensor", "vector", "scalar"):
            fns = self.q[name]
            getattr(block, name)(lambda e, fns=fns: [f(e) for f in fns])


def build_program(taps=(), phases=9.0):
    nc = bass.Bass("TRN2", target_bir_lowering=False, debug=False)
    ctx = contextlib.ExitStack()

    di = lambda n, s, dt=F32: nc.dram_tensor(n, s, dt, kind="ExternalInput").ap()
    do = lambda n, s, dt=F32: nc.dram_tensor(n, s, dt, kind="ExternalOutput").ap()

    x_in = di("x", [T_CORE, D])
    xT_in = di("xT", [D, T_CORE])
    w_qkv = di("w_qkv", [D, 3 * D])
    w_out_in = di("w_out", [D, D])
    gate_wT = di("gate_wT", [D, NE])
    w1s = di("w1s", [NE, NS_E, 8, 128, 256])
    ws1s = di("ws1s", [NS_S, 8, 128, 256])
    w2T_in = di("w2T", [NE, H_E, D])
    ws2T_in = di("ws2T", [H_S, D])
    consts_in = di("consts", [8, T_CORE])
    signcol = di("signcol", [128, 1])
    pperm = di("pperm", [128, 128], BF16)
    sel16 = di("sel16", [16, 8 * 128], BF16)
    ident = di("ident", [128, 128], BF16)
    esel = di("esel", [8, 8 * 128], BF16)

    out_t = do("out", [T_CORE, D])

    xn32_h = nc.dram_tensor("xn32_h", [T_CORE, D], F32, kind="Internal")
    xnb_h = nc.dram_tensor("xnb_h", [T_CORE, D], BF16, kind="Internal")
    act_h = nc.dram_tensor("act_h", [N_UNITS, 128, T_CORE], BF16,
                           kind="Internal")
    cc_in = nc.dram_tensor("cc_in", [128, 1040], F32, kind="Internal")
    cc_out = nc.dram_tensor("cc_out", [128, 1040], F32, kind="Internal")

    tap_aps = {}
    for t in taps:
        shape, dt = {
            "q_feat": ([128, 8 * T_CORE], BF16), "kv_full": ([128, 1040], F32),
            "attn": ([128, 8 * T_CORE], BF16), "xn": ([T_CORE, D], F32),
            "scores": ([128, 128], F32), "wrow": ([16, T_CORE], BF16),
            "cosk": ([128, N_TILES * HD], F32),
            "sink": ([128, N_TILES * HD], F32),
            "cosq": ([128, T_CORE], F32), "sinq": ([128, T_CORE], F32),
            "den": ([16, T_CORE], F32), "recip": ([16, T_CORE], BF16),
            "acth": ([4, 128, T_CORE], BF16),
            "xn2": ([T_CORE, D], F32),
            "res0": ([1, 128, D], BF16),
        }[t]
        tap_aps[t] = do("tap_" + t, shape, dt)

    with ctx:
        arena_t = ctx.enter_context(
            nc.sbuf_tensor("arena", [128, ARENA_COLS], BF16))
        A = Arena(arena_t, ARENA_COLS)
        p = P(nc, ctx)
        p.mksem("s_in", "s_gx", "s_gw", "s_gw2", "s_str", "s_w2l", "s_w2f",
                "s_xt",
                "s_xnl", "s_act", "s_st", "s_cc", "s_coll", "s_out",
                "s_mm", "s_vv", "s_ss", "s_tab", "s_ev", "s_qf",
                "s_kfree", "s_vfree", "s_kvin", "s_kvmm", "s_varg", "s_exp",
                "s_qbf", "s_qfree", "s_rotfree", "s_bev", "s_vmul",
                "s_res", "s_sg", "s_p1", "s_actst", "s_zdone", "s_vout",
                "s_gatev", "s_wr")

        scr_a = A.alloc("scr_a", T_CORE, F32)
        scr_b = A.alloc("scr_b", T_CORE, F32)
        sg_scr = A.alloc("sg_scr", T_CORE, F32)
        scr_i = sg_scr.bitcast(I32)
        ln_st = A.alloc("ln_st", 16, F32)
        ones_r = A.alloc("ones_r", 128, rows=1)
        ones_c = A.alloc("ones_c", 8)
        ident_sb = A.alloc("ident_sb", 128)
        sel16_sb = A.alloc("sel16_sb", 8 * 128, rows=16)
        x_fm = A.alloc("x_fm", 8 * T_CORE)
        wqkv_sb = A.alloc("wqkv", 8 * 3072)
        q_feat = A.alloc("q_feat", 8 * T_CORE)
        rows_c = A.alloc("rows_c", T_CORE, F32)
        ivf_r = A.alloc("ivf_r", 512, F32, rows=1)
        sign_sb = A.alloc("sign_sb", 8, F32)
        pperm_sb = A.alloc("pperm_sb", 128)
        cos_k = A.alloc("cos_k", N_TILES * HD, F32)
        sink_s = A.alloc("sink_s", N_TILES * HD, F32)
        cos_q = A.alloc("cos_q", T_CORE, F32)
        sinq_s = A.alloc("sinq_s", T_CORE, F32)
        kfeat_t = A.alloc("kfeat", 1024)
        v_bf = A.alloc("v_bf", 1024)
        q_bf = A.alloc("q_bf", 1024)
        kv_sb = A.alloc("kv_sb", 1040, F32)
        kv_full = A.alloc("kv_full", 1040, F32)
        kv_bf = A.alloc("kv_bf", 1024)
        ksum16 = A.alloc("ksum16", 8 * 16)

        trow_r = rows_c[0:1, :]
        ifhi_r = ivf_r[0:1, 0:HD]
        iflo_r = ivf_r[0:1, HD:2 * HD]
        if128h_r = ivf_r[0:1, 128:256]
        if128l_r = ivf_r[0:1, 256:384]

        pA = ctx.enter_context(nc.psum_tensor("pA", [128, 2048], F32))
        pB = ctx.enter_context(nc.psum_tensor("pB", [128, 1024], F32))
        pCb = ctx.enter_context(nc.psum_tensor("pCb", [128, 2048], BF16))
        pC = pCb[:].bitcast(F32)

        def ld(eng, dst, src, sem):
            p.op(eng, lambda e, d=dst, s=src: e.dma_start(out=d, in_=s),
                 inc=sem, inc_n=16)

        def mm(out, lhsT, rhs, st, sp):
            return lambda e: e.matmul(out, lhsT, rhs, start=st, stop=sp,
                                      skip_group_check=True)

        def finalize():
            tap_srcs = {
                "q_feat": lambda: q_feat, "attn": lambda: attn_fm,
                "kv_full": lambda: kv_full, "cosk": lambda: cos_k,
                "sink": lambda: sink_s, "cosq": lambda: cos_q,
                "sinq": lambda: sinq_s, "den": lambda: den_r,
                "recip": lambda: recip_r, "scores": lambda: scores_sb,
                "wrow": lambda: wrow,
            }
            L = locals()
            for s in list(p.sem):
                cnt = p.c(s)
                if cnt and s not in ("s_st", "s_out"):
                    p.wait("sync", s, cnt)
            for t in taps:
                if t == "res0":
                    ld("sync", tap_aps[t][0], dbg_t, "s_st")
                elif t == "acth":
                    for ii, u in enumerate((0, 5, 30, 48)):
                        ld("sync", tap_aps[t][ii], act_h.ap()[u], "s_st")
                elif t == "xn":
                    ld("sync", tap_aps[t], xn32_h.ap(), "s_st")
                elif t in tap_srcs:
                    try:
                        srcv = tap_srcs[t]()
                    except NameError:
                        continue
                    if t == "wrow":
                        ld("sync", tap_aps[t][0:8, :], srcv[0:8, :], "s_st")
                    else:
                        ld("sync", tap_aps[t], srcv, "s_st")
            if p.c("s_st"):
                p.wait("sync", "s_st", p.c("s_st"))
            if p.c("s_out"):
                p.wait("sync", "s_out", p.c("s_out"))
            block = ctx.enter_context(nc.Block())
            p.emit(block)
            return nc

        ld("sync", rows_c[0:1, :], consts_in[0:1, :], "s_in")
        ld("sync", ivf_r[0:1, 0:512], consts_in[1:2, 0:512], "s_in")
        ld("sync", sign_sb[:, 0:1], signcol, "s_in")
        ld("sync", pperm_sb, pperm, "s_in")
        ld("sync", sel16_sb, sel16, "s_in")
        ld("sync", ident_sb, ident, "s_in")
        n_in = p.c("s_in")

        for c in range(8):
            ld("gpsimd", x_fm[:, c * T_CORE:(c + 1) * T_CORE],
               xT_in[c * 128:(c + 1) * 128, :], "s_gx")
        gx_all = p.c("s_gx")
        for c in range(8):
            ld("gpsimd", wqkv_sb[:, c * 3072 + 1024:c * 3072 + 3072],
               w_qkv[c * 128:(c + 1) * 128, 1024:3072], "s_gw")
        gw_kv = p.c("s_gw")
        for c in range(8):
            ld("gpsimd", wqkv_sb[:, c * 3072:c * 3072 + 1024],
               w_qkv[c * 128:(c + 1) * 128, 0:1024], "s_gw")
        gw_all = p.c("s_gw")

        if phases <= 0.2:
            return finalize()

        p.wait("tensor", "s_in", n_in)
        for i in range(N_TILES):
            sl = trow_r[:, i * 128:(i + 1) * 128]
            p.op("tensor", mm(pB[:, i * HD:(i + 1) * HD], sl, ifhi_r,
                              i % 8 == 0, False))
            p.op("tensor", mm(pB[:, i * HD:(i + 1) * HD], sl, iflo_r,
                              False, True),
                 inc="s_mm" if i == N_TILES - 1 else None)
        mm_ktab = p.c("s_mm")
        for q4 in range(4):
            sl = trow_r[:, q4 * 512:(q4 + 1) * 512]
            p.op("tensor", mm(pA[:, q4 * 512:(q4 + 1) * 512], if128h_r, sl,
                              True, False))
            p.op("tensor", mm(pA[:, q4 * 512:(q4 + 1) * 512], if128l_r, sl,
                              False, True), inc="s_mm" if q4 == 3 else None)
        mm_qtab = p.c("s_mm")
        if phases <= 0.4:
            return finalize()

        p.wait("vector", "s_in", n_in)
        p.op("vector", lambda e: e.memset(ones_r, 1.0))
        p.op("vector", lambda e: e.memset(ones_c[:, 0:1], 1.0))
        p.op("vector", lambda e: e.memset(kv_sb, 0.0))
        p.op("vector", lambda e: e.memset(ksum16, 0.0))

        def range_reduce(v, src, n, r_dst, rc_dst):
            kf = scr_a[:, 0:n]
            v.tensor_single_scalar(kf, src, INV_2PI, ALU.mult)
            v.tensor_copy(scr_i[:, 0:n], kf)
            v.tensor_copy(kf, scr_i[:, 0:n])
            v.scalar_tensor_tensor(r_dst, kf, -C1, src, ALU.mult, ALU.add)
            v.scalar_tensor_tensor(r_dst, kf, -C2, r_dst, ALU.mult, ALU.add)
            v.tensor_single_scalar(rc_dst, r_dst, PI / 2, ALU.add)
            v.tensor_single_scalar(kf, rc_dst, PI, ALU.is_gt)
            return v.scalar_tensor_tensor(rc_dst, kf, -TWO_PI, rc_dst,
                                          ALU.mult, ALU.add)

        nk = N_TILES * HD
        p.wait("vector", "s_mm", mm_ktab)
        p.op("vector", lambda e: range_reduce(e, pB[:, 0:nk], nk,
                                              scr_b[:, 0:nk], pB[:, 0:nk]),
             inc="s_vv")
        vv_kt = p.c("s_vv")
        p.wait("vector", "s_mm", mm_qtab)
        p.op("vector", lambda e: range_reduce(e, pA[:, 0:T_CORE], T_CORE,
                                              scr_b[:, 0:T_CORE],
                                              pA[:, 0:T_CORE]), inc="s_vv")
        vv_qt = p.c("s_vv")
        if phases <= 0.6:
            return finalize()

        p.wait("scalar", "s_vv", vv_kt)
        p.op("scalar", lambda e: e.activation(sink_s, scr_b[:, 0:nk], AF.Sin))
        p.op("scalar", lambda e: e.activation(cos_k, pB[:, 0:nk], AF.Sin),
             inc="s_ss")
        ss_kt = p.c("s_ss")
        p.wait("scalar", "s_vv", vv_qt)
        p.op("scalar", lambda e: e.activation(sinq_s, scr_b[:, 0:T_CORE],
                                              AF.Sin))
        p.op("scalar", lambda e: e.activation(cos_q, pA[:, 0:T_CORE], AF.Sin),
             inc="s_ss")
        ss_qt = p.c("s_ss")
        if phases <= 0.8:
            return finalize()

        p.wait("vector", "s_ss", ss_kt)
        sk4 = sink_s.rearrange("p (i a b) -> p i a b", i=N_TILES, a=2)
        p.op("vector", lambda e: e.tensor_single_scalar(
            sk4[:, :, 0, :], sk4[:, :, 0, :], -1.0, ALU.mult), inc="s_tab")
        tab_k = p.c("s_tab")
        p.wait("vector", "s_ss", ss_qt)
        p.op("vector", lambda e: e.tensor_single_scalar(cos_q, cos_q, 0.125,
                                                        ALU.mult))
        p.op("vector", lambda e: e.tensor_scalar(sinq_s, sinq_s,
                                                 sign_sb[:, 0:1], 0.125,
                                                 ALU.mult, ALU.mult),
             inc="s_tab")
        tab_q = p.c("s_tab")
        if phases <= 1:
            return finalize()

        p.wait("tensor", "s_gx", gx_all)
        p.wait("tensor", "s_gw", gw_kv)
        p.wait("tensor", "s_tab", tab_k)
        mm_1a0 = p.c("s_mm")
        for i in range(N_TILES):
            if i > 0:
                p.wait("tensor", "s_kfree", i)
            for kc in range(KC):
                lhs = x_fm[:, kc * T_CORE + i * 128:kc * T_CORE + (i + 1) * 128]
                for hf in range(2):
                    p.op("tensor", mm(pB[:, hf * 512:(hf + 1) * 512], lhs,
                                      wqkv_sb[:, kc * 3072 + 1024 + hf * 512:
                                              kc * 3072 + 1024 + (hf + 1) * 512],
                                      kc == 0, kc == KC - 1),
                         inc="s_mm" if kc == KC - 1 and hf == 1 else None)
            if i > 0:
                p.wait("tensor", "s_vfree", i)
            for kc in range(KC):
                lhs = x_fm[:, kc * T_CORE + i * 128:kc * T_CORE + (i + 1) * 128]
                for hf in range(2):
                    p.op("tensor", mm(pC[:, hf * 512:(hf + 1) * 512], lhs,
                                      wqkv_sb[:, kc * 3072 + 2048 + hf * 512:
                                              kc * 3072 + 2048 + (hf + 1) * 512],
                                      kc == 0, kc == KC - 1),
                         inc="s_mm" if kc == KC - 1 and hf == 1 else None)
            p.wait("tensor", "s_kvin", i + 1)
            for c in range(8):
                lhs = kfeat_t[:, c * 128:(c + 1) * 128]
                p.op("tensor", mm(pA[:, c * 128:(c + 1) * 128], lhs,
                                  v_bf[:, c * 128:(c + 1) * 128],
                                  i == 0 and c in (0, 4), i == N_TILES - 1))
                p.op("tensor", mm(pA[:, 1024 + c * 16:1024 + c * 16 + 1], lhs,
                                  ones_c[:, 0:1], i == 0 and c == 0,
                                  i == N_TILES - 1),
                     inc=("s_kvmm" if c == 7 and i < N_TILES - 1 else
                          "s_mm" if c == 7 and i == N_TILES - 1 else None))
        mm_kv_done = p.c("s_mm")

        kp4 = pB[:].rearrange("p (h a b) -> p h a b", h=16, a=2)
        kp3 = pB[:].rearrange("p (h d) -> p h d", h=16)
        for i in range(N_TILES):
            cs_b = cos_k[:, i * HD:(i + 1) * HD].unsqueeze(1).broadcast_to(
                [128, 16, HD])
            sn = sink_s[:, i * HD:(i + 1) * HD].rearrange(
                "p (a b) -> p a b", a=2)
            sn0 = sn[:, 0, :].unsqueeze(1).broadcast_to([128, 16, 32])
            sn1 = sn[:, 1, :].unsqueeze(1).broadcast_to([128, 16, 32])
            tc = scr_a[:, 0:1024]
            ts = scr_b[:, 0:1024]
            tc3 = tc.rearrange("p (h d) -> p h d", h=16)
            ts4 = ts.rearrange("p (h a b) -> p h a b", h=16, a=2)
            p.wait("vector", "s_mm", mm_1a0 + 2 * i + 1)
            p.op("vector", lambda e, a=tc3, c=cs_b:
                 e.tensor_tensor(a, kp3, c, ALU.mult))
            p.op("vector", lambda e, a=ts4, c=sn0:
                 e.tensor_tensor(a[:, :, 0, :], kp4[:, :, 1, :], c, ALU.mult))
            p.op("vector", lambda e, a=ts4, c=sn1:
                 e.tensor_tensor(a[:, :, 1, :], kp4[:, :, 0, :], c, ALU.mult),
                 inc="s_kfree")
            p.op("vector", lambda e, a=tc, b=ts:
                 e.tensor_tensor(a, a, b, ALU.add))
            p.op("vector", lambda e, a=ts, b=tc:
                 e.tensor_single_scalar(a, b, 0.0, ALU.min), inc="s_varg")
            p.wait("vector", "s_mm", mm_1a0 + 2 * i + 2)
            if i > 0:
                p.wait("vector", "s_kvmm", i)
            p.op("vector", lambda e: e.tensor_copy(v_bf, pC[:]),
                 inc="s_vfree")
            p.wait("vector", "s_exp", i + 1)
            p.op("vector", lambda e, a=tc: e.scalar_tensor_tensor(
                kfeat_t, a, 0.0, sg_scr[:, 0:1024], ALU.max, ALU.add),
                inc="s_kvin")
        for i in range(N_TILES):
            p.wait("scalar", "s_varg", i + 1)
            p.op("scalar", lambda e: e.activation(sg_scr[:, 0:1024],
                                                  scr_b[:, 0:1024], AF.Exp),
                 inc="s_exp")

        p.wait("vector", "s_mm", mm_kv_done)

        def kv_evac(e):
            for c in range(8):
                e.tensor_copy(kv_sb[0:64, c * 128:c * 128 + 64],
                              pA[0:64, c * 128:c * 128 + 64])
                e.tensor_copy(kv_sb[64:128, c * 128 + 64:c * 128 + 128],
                              pA[64:128, c * 128 + 64:c * 128 + 128])
                e.tensor_copy(kv_sb[:, 1024 + 2 * c:1024 + 2 * c + 1],
                              pA[:, 1024 + c * 16:1024 + c * 16 + 1])
            return e.tensor_copy(kv_sb[0:1, 0:1], kv_sb[0:1, 0:1])
        p.op("vector", kv_evac, inc="s_ev")
        ev_kv = p.c("s_ev")

        p.wait("sync", "s_ev", ev_kv)
        ld("sync", cc_in.ap(), kv_sb, "s_cc")
        cc1 = p.c("s_cc")
        p.wait("gpsimd", "s_cc", cc1)
        p.op("gpsimd", lambda e: e.collective_compute(
            "AllReduce", ALU.add, replica_groups=[[0, 1, 2, 3], [4, 5, 6, 7]],
            ins=[cc_in.ap()], outs=[cc_out.ap()]), inc="s_coll")
        p.wait("sync", "s_coll", 1)
        ld("sync", kv_full, cc_out.ap(), "s_cc")
        cc2 = p.c("s_cc")

        def ksum_build(e):
            for c in range(8):
                e.tensor_copy(ksum16[0:64, c * 16 + 2 * c:c * 16 + 2 * c + 1],
                              kv_full[0:64, 1024 + 2 * c:1024 + 2 * c + 1])
                e.tensor_copy(
                    ksum16[64:128, c * 16 + 2 * c + 1:c * 16 + 2 * c + 2],
                    kv_full[64:128, 1024 + 2 * c:1024 + 2 * c + 1])
            return e.tensor_copy(ksum16[0:1, 0:1], ksum16[0:1, 0:1])
        if phases <= 2:
            return finalize()

        p.wait("tensor", "s_gw", gw_all)
        p.wait("tensor", "s_tab", tab_q)
        p.wait("tensor", "s_ev", ev_kv)
        p.wait("tensor", "s_kfree", N_TILES)
        p.wait("tensor", "s_vfree", N_TILES)
        mm_q0 = p.c("s_mm")
        for idx in range(16):
            m, hf = idx // 2, idx % 2
            pq = pB if hf == 0 else pC
            if idx >= 2:
                p.wait("tensor", "s_qfree", idx - 1)
            for kc in range(KC):
                lhs = wqkv_sb[:, kc * 3072 + m * 128:kc * 3072 + (m + 1) * 128]
                for pp in range(2):
                    p.op("tensor", mm(
                        pq[:, pp * 512:(pp + 1) * 512], lhs,
                        x_fm[:, kc * T_CORE + hf * 1024 + pp * 512:
                             kc * T_CORE + hf * 1024 + (pp + 1) * 512],
                        kc == 0, kc == KC - 1),
                        inc="s_mm" if kc == KC - 1 and pp == 1 else None)
            p.wait("tensor", "s_qbf", idx + 1)
            if idx >= 2:
                p.wait("tensor", "s_rotfree", idx - 1)
            pr = pA[:, (idx % 2) * 1024:(idx % 2) * 1024 + 1024]
            for pp in range(2):
                p.op("tensor", mm(pr[:, pp * 512:(pp + 1) * 512], pperm_sb,
                                  q_bf[:, pp * 512:(pp + 1) * 512],
                                  True, True),
                     inc="s_mm" if pp == 1 else None)

        for idx in range(16):
            m, hf = idx // 2, idx % 2
            pq = pB if hf == 0 else pC
            tsl = slice(hf * 1024, (hf + 1) * 1024)
            pr = pA[:, (idx % 2) * 1024:(idx % 2) * 1024 + 1024]
            p.wait("vector", "s_mm", mm_q0 + 2 * idx + 1)
            p.op("vector", lambda e, a=pq: e.tensor_copy(q_bf, a[:]),
                 inc="s_qbf")
            p.op("vector", lambda e, a=pq, t=tsl: e.tensor_tensor(
                scr_a[:, 0:1024], a[:], cos_q[:, t], ALU.mult), inc="s_qfree")
            p.wait("vector", "s_mm", mm_q0 + 2 * idx + 2)
            p.op("vector", lambda e, a=pr, t=tsl: e.tensor_tensor(
                scr_b[:, 0:1024], a, sinq_s[:, t], ALU.mult), inc="s_rotfree")
            p.op("vector", lambda e: e.tensor_tensor(
                scr_a[:, 0:1024], scr_a[:, 0:1024], scr_b[:, 0:1024],
                ALU.add))
            p.op("vector", lambda e: e.tensor_single_scalar(
                scr_b[:, 0:1024], scr_a[:, 0:1024], 0.0, ALU.min),
                inc="s_varg")
            p.wait("vector", "s_exp", 16 + idx + 1)
            p.op("vector", lambda e, m=m, hf=hf: e.scalar_tensor_tensor(
                q_feat[:, m * T_CORE + hf * 1024:m * T_CORE + (hf + 1) * 1024],
                scr_a[:, 0:1024], 0.0, sg_scr[:, 1024:2048], ALU.max,
                ALU.add), inc="s_qf")
        for idx in range(16):
            p.wait("scalar", "s_varg", 16 + idx + 1)
            p.op("scalar", lambda e: e.activation(sg_scr[:, 1024:2048],
                                                  scr_b[:, 0:1024], AF.Exp),
                 inc="s_exp")
        p.wait("vector", "s_cc", cc2)
        p.op("vector", lambda e: e.tensor_copy(kv_bf, kv_full[:, 0:1024]))
        p.op("vector", ksum_build, inc="s_ev")
        ev_kvbf = p.c("s_ev")
        qf_all = p.c("s_qf")
        mm_1b_done = p.c("s_mm")
        if phases <= 3:
            return finalize()

        A.free("wqkv", "cos_k", "sink_s", "rows_c", "sign_sb", "ivf_r")
        attn_fm = A.alloc("attn_fm", 8 * T_CORE)
        wout_sb = A.alloc("wout_sb", 8 * 1024)
        gate_sb = A.alloc("gate_sb", 64)
        den_r = A.alloc("den_r", T_CORE, F32, rows=16)
        recip_r = A.alloc("recip_r", T_CORE, rows=16)
        p.wait("gpsimd", "s_mm", mm_1b_done)
        for c in range(8):
            ld("gpsimd", wout_sb[:, c * 1024:(c + 1) * 1024],
               w_out_in[c * 128:(c + 1) * 128, :], "s_gw2")
        for c in range(8):
            ld("gpsimd", gate_sb[:, c * 8:(c + 1) * 8],
               gate_wT[c * 128:(c + 1) * 128, :], "s_gw2")
        gw2_all = p.c("s_gw2")

        p.wait("tensor", "s_ev", ev_kvbf)
        p.wait("tensor", "s_qf", qf_all)
        p.wait("tensor", "s_rotfree", 16)
        for pp in range(4):
            for c in range(8):
                p.op("tensor", mm(
                    pA[0:16, pp * 512:(pp + 1) * 512],
                    ksum16[:, c * 16:(c + 1) * 16],
                    q_feat[:, c * T_CORE + pp * 512:c * T_CORE + (pp + 1) * 512],
                    c == 0, c == 7),
                    inc="s_mm" if pp == 3 and c == 7 else None)
        mm_den = p.c("s_mm")
        p.wait("vector", "s_mm", mm_den)
        p.op("vector", lambda e: e.tensor_single_scalar(
            den_r, pA[0:16, 0:T_CORE], DEN_EPS, ALU.max), inc="s_vv")
        vv_den = p.c("s_vv")
        p.wait("scalar", "s_vv", vv_den)
        p.op("scalar", lambda e: e.activation(den_r, den_r, AF.Ln))
        p.op("scalar", lambda e: e.activation(recip_r, den_r, AF.Exp,
                                              scale=-1.0), inc="s_ss")
        ss_recip = p.c("s_ss")

        p.wait("tensor", "s_ss", ss_recip)
        mm_p2 = p.c("s_mm")
        bev_p2 = p.c("s_bev")
        vmul_p2 = p.c("s_vmul")
        for c in range(8):
            for pp in range(4):
                idx = c * 4 + pp
                if idx >= 4:
                    p.wait("tensor", "s_vmul", vmul_p2 + idx - 3)
                p.op("tensor", mm(
                    pA[:, pp * 512:(pp + 1) * 512],
                    kv_bf[:, c * 128:(c + 1) * 128],
                    q_feat[:, c * T_CORE + pp * 512:c * T_CORE + (pp + 1) * 512],
                    True, True))
                if idx >= 2:
                    p.wait("tensor", "s_bev", bev_p2 + idx - 1)
                p.op("tensor", mm(
                    pB[:, (pp % 2) * 512:(pp % 2) * 512 + 512],
                    sel16_sb[:, c * 128:(c + 1) * 128],
                    recip_r[:, pp * 512:(pp + 1) * 512], True, True),
                    inc="s_mm")
        wb2 = scr_b
        for c in range(8):
            for pp in range(4):
                idx = c * 4 + pp
                sl = slice((idx % 2) * 512, (idx % 2) * 512 + 512)
                p.wait("scalar", "s_mm", mm_p2 + idx + 1)
                p.op("scalar", lambda e, s=sl, pp=pp:
                     e.copy(wb2[:, s], pB[:, (pp % 2) * 512:
                                          (pp % 2) * 512 + 512]),
                     inc="s_bev")
                p.wait("vector", "s_bev", bev_p2 + idx + 1)
                p.op("vector", lambda e, c=c, pp=pp, s=sl: e.tensor_tensor(
                    attn_fm[:, c * T_CORE + pp * 512:
                            c * T_CORE + (pp + 1) * 512],
                    pA[:, pp * 512:(pp + 1) * 512], wb2[:, s], ALU.mult),
                    inc="s_vmul")
        vmul_all = p.c("s_vmul")
        if phases <= 4:
            return finalize()

        A.free("q_bf", "kv_sb", "kv_full", "cos_q", "sinq_s", "pperm_sb")
        dbg_t = A.alloc("dbg_t", D, BF16) if "res0" in taps else None
        x_tile = A.alloc("x_tile", 2 * D, F32)
        res_t = A.alloc("res_t", 2 * D, F32)
        xnb_t = A.alloc("xnb_t", 2 * D)

        p.wait("tensor", "s_vmul", vmul_all)
        p.wait("tensor", "s_bev", bev_p2 + 32)
        mm_p3 = p.c("s_mm")
        for i in range(N_TILES):
            pz = pB if i % 2 == 0 else pC
            if i >= 2:
                p.wait("tensor", "s_res", i - 1)
            for kc in range(KC):
                lhs = attn_fm[:, kc * T_CORE + i * 128:
                              kc * T_CORE + (i + 1) * 128]
                for hf in range(2):
                    p.op("tensor", mm(pz[:, hf * 512:(hf + 1) * 512], lhs,
                                      wout_sb[:, kc * 1024 + hf * 512:
                                              kc * 1024 + (hf + 1) * 512],
                                      kc == 0, kc == KC - 1),
                         inc="s_mm" if kc == KC - 1 and hf == 1 else None)

        import os as _os
        _NO_ACC = _os.environ.get("K_NO_ACCUM") == "1"

        def ln_stats(e, src, o, scr):
            e.reduce_sum(ln_st[:, o:o + 1], src, AX.X)
            if _NO_ACC:
                e.tensor_tensor(scr[:, 0:D], src, src, ALU.mult)
                e.reduce_sum(ln_st[:, o + 1:o + 2], scr[:, 0:D], AX.X)
            else:
                e.scalar_tensor_tensor(scr[:, 0:D], src, 1.0, src, ALU.mult,
                                       ALU.mult,
                                       accum_out=ln_st[:, o + 1:o + 2])
            e.tensor_single_scalar(ln_st[:, o + 2:o + 3],
                                   ln_st[:, o:o + 1], 1.0 / D, ALU.mult)
            e.tensor_single_scalar(ln_st[:, o + 1:o + 2],
                                   ln_st[:, o + 1:o + 2], 1.0 / D, ALU.mult)
            e.tensor_tensor(ln_st[:, o + 3:o + 4], ln_st[:, o + 2:o + 3],
                            ln_st[:, o + 2:o + 3], ALU.mult)
            e.tensor_tensor(ln_st[:, o + 1:o + 2], ln_st[:, o + 1:o + 2],
                            ln_st[:, o + 3:o + 4], ALU.subtract)
            return e.tensor_single_scalar(ln_st[:, o + 1:o + 2],
                                          ln_st[:, o + 1:o + 2], EPS_LN,
                                          ALU.add)

        vv_p3_0 = p.c("s_vv")

        def p3_store(i):
            p.wait("sync", "s_vv", vv_p3_0 + i + 1)
            ld("sync", xn32_h.ap()[i * 128:(i + 1) * 128, :],
               res_t[:, (i % 2) * D:(i % 2 + 1) * D], "s_st")
            ld("sync", xnb_h.ap()[i * 128:(i + 1) * 128, :],
               xnb_t[:, (i % 2) * D:(i % 2 + 1) * D], "s_st")

        st_p3_base = p.c("s_st")
        for i in range(N_TILES):
            if i >= 2:
                p.wait("sync", "s_res", i - 1)
            ld("sync", x_tile[:, (i % 2) * D:(i % 2 + 1) * D],
               x_in[i * 128:(i + 1) * 128, :], "s_xt")
            if i >= 2:
                p3_store(i - 2)
        p3_store(N_TILES - 2)
        p3_store(N_TILES - 1)

        for i in range(N_TILES):
            pz = pB if i % 2 == 0 else pC
            xs = x_tile[:, (i % 2) * D:(i % 2 + 1) * D]
            rs = res_t[:, (i % 2) * D:(i % 2 + 1) * D]
            p.wait("vector", "s_mm", mm_p3 + i + 1)
            p.wait("vector", "s_xt", 16 * (i + 1))
            if i >= 2:
                p.wait("vector", "s_st", 32 * (i - 1))
            p.op("vector", lambda e, a=rs, b=xs, c=pz:
                 e.tensor_tensor(a, b, c[:], ALU.add), inc="s_res")
            if dbg_t is not None and i == 0:
                p.op("vector", lambda e, a=rs: e.tensor_copy(dbg_t, a))
            p.op("vector", lambda e, a=rs: ln_stats(e, a, 0, scr_b),
                 inc="s_varg")
            p.wait("vector", "s_exp", 32 + i + 1)
            p.op("vector", lambda e, a=rs: e.tensor_scalar(
                a, a, ln_st[:, 2:3], ln_st[:, 4:5], ALU.subtract, ALU.mult))
            p.op("vector", lambda e, a=rs, i=i: e.tensor_copy(
                xnb_t[:, (i % 2) * D:(i % 2 + 1) * D], a), inc="s_vv")
        for i in range(N_TILES):
            p.wait("scalar", "s_varg", 32 + i + 1)
            p.op("scalar", lambda e: e.activation(ln_st[:, 4:5],
                                                  ln_st[:, 1:2], AF.Ln))
            p.op("scalar", lambda e: e.activation(ln_st[:, 4:5],
                                                  ln_st[:, 4:5], AF.Exp,
                                                  scale=-0.5), inc="s_exp")
        st_xn = p.c("s_st")
        if "xn2" in taps:
            p.wait("sync", "s_st", st_xn)
            ld("sync", tap_aps["xn2"], xn32_h.ap(), "s_st")
            st_xn = p.c("s_st")
        if phases <= 4.9:
            return finalize()

        p.wait("scalar", "s_st", st_xn)
        for c in range(8):
            p.op("scalar", lambda e, c=c: e.dma_start_transpose(
                x_fm[:, c * T_CORE:(c + 1) * T_CORE],
                xnb_h.ap()[:, c * 128:(c + 1) * 128]), inc="s_xnl", inc_n=16)
        xnl_all = p.c("s_xnl")
        xn_fm = x_fm
        if phases <= 5:
            return finalize()

        A.free("den_r", "recip_r", "kfeat", "v_bf")
        scores_sb = A.alloc("scores", 128, F32)
        esel_sb = A.alloc("esel_sb", 8 * 128, rows=8)
        gst = A.alloc("gst", 64, F32)
        w_tmb = A.alloc("w_tmb", 128)
        wrow = A.alloc("wrow", T_CORE, rows=16)

        ld("sync", esel_sb[0:8, :], esel, "s_in")
        n_in2 = p.c("s_in")
        p.wait("tensor", "s_xnl", xnl_all)
        p.wait("tensor", "s_gw2", gw2_all)
        p.wait("tensor", "s_res", 16)
        for i in range(N_TILES):
            for kc in range(KC):
                lhs = xn_fm[:, kc * T_CORE + i * 128:kc * T_CORE + (i + 1) * 128]
                p.op("tensor", mm(pA[:, i * 8:(i + 1) * 8], lhs,
                                  gate_sb[:, kc * 8:(kc + 1) * 8],
                                  kc == 0, kc == KC - 1),
                     inc="s_mm" if i == N_TILES - 1 and kc == KC - 1 else None)
        mm_gate = p.c("s_mm")
        p.wait("vector", "s_mm", mm_gate)

        def gate_v1(e):
            s3 = scores_sb.rearrange("p (i g) -> p i g", g=8)
            e.tensor_copy(scores_sb, pA[:, 0:128])
            e.tensor_reduce(gst[:, 0:16], s3, AX.X, ALU.max)
            g16 = gst[:, 0:16].unsqueeze(2).broadcast_to([128, 16, 8])
            return e.tensor_tensor(s3, s3, g16, ALU.subtract)
        p.op("vector", gate_v1, inc="s_vv")
        vv_g1 = p.c("s_vv")
        p.wait("scalar", "s_vv", vv_g1)
        p.op("scalar", lambda e: e.activation(scores_sb, scores_sb, AF.Exp),
             inc="s_ss")
        ss_g = p.c("s_ss")
        p.wait("vector", "s_ss", ss_g)

        def gate_v2(e):
            s3 = scores_sb.rearrange("p (i g) -> p i g", g=8)
            a3 = scr_a[:, 0:128].rearrange("p (i g) -> p i g", g=8)
            b3 = scr_b[:, 0:128].rearrange("p (i g) -> p i g", g=8)
            e.tensor_reduce(gst[:, 0:16], s3, AX.X, ALU.add)
            e.tensor_copy(gst[:, 48:64], gst[:, 0:16])
            e.reciprocal(gst[:, 16:32], gst[:, 0:16])
            e.tensor_copy(gst[:, 48:64], gst[:, 0:16])
            g16 = gst[:, 16:32].unsqueeze(2).broadcast_to([128, 16, 8])
            e.tensor_tensor(s3, s3, g16, ALU.mult)
            e.tensor_reduce(gst[:, 0:16], s3, AX.X, ALU.max)
            m16 = gst[:, 0:16].unsqueeze(2).broadcast_to([128, 16, 8])
            e.tensor_tensor(a3, s3, m16, ALU.is_ge)
            e.scalar_tensor_tensor(scr_b[:, 0:128], scr_a[:, 0:128], -2.0,
                                   scores_sb, ALU.mult, ALU.add)
            e.tensor_reduce(gst[:, 32:48], b3, AX.X, ALU.max)
            m2 = gst[:, 32:48].unsqueeze(2).broadcast_to([128, 16, 8])
            e.tensor_tensor(b3, s3, m2, ALU.is_ge)
            e.tensor_tensor(scores_sb, scores_sb, scr_b[:, 0:128], ALU.mult)
            return e.tensor_copy(w_tmb, scores_sb)
        p.op("vector", gate_v2, inc="s_gatev")
        p.wait("tensor", "s_gatev", 1)
        for i in range(N_TILES):
            p.op("tensor", lambda e, i=i: e.matmul(
                pCb[0:8, i * 128:(i + 1) * 128],
                w_tmb[:, i * 8:(i + 1) * 8], ident_sb, start=True, stop=True,
                is_transpose=True, skip_group_check=True), inc="s_mm")
        mm_wr = p.c("s_mm")
        p.wait("vector", "s_mm", mm_wr)
        p.op("vector", lambda e: e.tensor_copy(wrow[0:8, :],
                                               pCb[0:8, 0:T_CORE]),
             inc="s_wr")

        if phases <= 6:
            return finalize()

        units = [("e", e_, s_) for e_ in range(NE) for s_ in range(NS_E)] + \
                [("s", 0, s_) for s_ in range(NS_S)]
        A.free("x_tile", "xnb_t", "kv_bf", "ksum16", "scores", "gst",
               "w_tmb", "res_t", "q_feat", "attn_fm", "wout_sb", "gate_sb",
               "sel16_sb", "ident_sb", "scr_a", "scr_b")
        stripe_w = A.alloc("stripe_w", 2 * 8 * 256)
        wb_bf = A.alloc("wb_bf", T_CORE)
        act_st = A.alloc("act_st", 2 * T_CORE)
        w2slots = {}
        n_w2_early = 48
        for u in range(48):
            w2slots[u] = A.alloc(f"w2_{u}", 1024)
        try:
            for u in range(48, N_UNITS):
                w2slots[u] = A.alloc(f"w2_{u}", 1024)
            n_w2_early = N_UNITS
        except MemoryError:
            pass
        w2stage = None
        if n_w2_early == N_UNITS:
            try:
                w2stage = A.alloc("w2stage", 1024, F32)
            except MemoryError:
                pass
        use_hw_w2 = w2stage is not None

        def w2src(u):
            if u < NE * NS_E:
                e_, s_ = u // NS_E, u % NS_E
                return w2T_in[e_, s_ * 128:(s_ + 1) * 128, :]
            s_ = u - NE * NS_E
            return ws2T_in[s_ * 128:(s_ + 1) * 128, :]

        p.wait("gpsimd", "s_wr", 1)
        p.wait("gpsimd", "s_st", st_xn)

        def emit_stripe_load(uix):
            kind, e_, s_ = units[uix]
            buf = uix % 2
            if uix >= 2:
                p.wait("gpsimd", "s_p1", uix - 1)
            src = w1s[e_, s_] if kind == "e" else ws1s[s_]
            dst = stripe_w[:, buf * 2048:(buf + 1) * 2048]
            p.op("gpsimd", lambda e, s=src, d=dst: e.dma_start(
                out=d.rearrange("p (j c) -> p j c", j=8),
                in_=s.transpose([1, 0, 2])), inc="s_str", inc_n=16)

        emit_stripe_load(0)
        emit_stripe_load(1)
        n_w2 = N_UNITS if use_hw_w2 else 0
        for uix in range(N_UNITS):
            if uix + 2 < N_UNITS:
                emit_stripe_load(uix + 2)
            if n_w2 < n_w2_early:
                ld("gpsimd", w2slots[n_w2], w2src(n_w2), "s_w2l")
                n_w2 += 1
        while n_w2 < n_w2_early:
            ld("gpsimd", w2slots[n_w2], w2src(n_w2), "s_w2l")
            n_w2 += 1

        if use_hw_w2:
            p.wait("sync", "s_wr", 1)
            p.wait("sync", "s_st", st_xn)
            ld("sync", w2stage[:, 0:1024], w2src(0), "s_w2f")

        p.wait("tensor", "s_wr", 1)
        st_xn_base = st_xn
        for uix, (kind, e_, s_) in enumerate(units):
            buf = uix % 2
            if kind == "e" and s_ == 0:
                if uix > 0:
                    p.wait("tensor", "s_sg", uix)
                p.wait("tensor", "s_in", n_in2)
                for pp in range(4):
                    p.op("tensor", mm(pA[:, pp * 512:(pp + 1) * 512],
                                      esel_sb[0:8, e_ * 128:(e_ + 1) * 128],
                                      wrow[0:8, pp * 512:(pp + 1) * 512],
                                      True, True),
                         inc="s_mm" if pp == 3 else None)
                mmv = p.c("s_mm")
                p.wait("scalar", "s_mm", mmv)
                if uix > 0:
                    p.wait("scalar", "s_actst", uix)
                p.op("scalar", lambda e: e.copy(wb_bf, pA[:, 0:T_CORE]),
                     inc="s_bev")
                p.wait("tensor", "s_bev", p.c("s_bev"))
            p.wait("tensor", "s_str", (uix + 1) * 16)
            if uix > 0 and not (kind == "e" and s_ == 0):
                p.wait("tensor", "s_sg", uix)
            for kc in range(KC):
                lhs = stripe_w[:, buf * 2048 + kc * 256 + 128:
                               buf * 2048 + kc * 256 + 256]
                for pp in range(4):
                    p.op("tensor", mm(pA[:, pp * 512:(pp + 1) * 512], lhs,
                                      xn_fm[:, kc * T_CORE + pp * 512:
                                            kc * T_CORE + (pp + 1) * 512],
                                      kc == 0, kc == KC - 1),
                         inc="s_mm" if kc == KC - 1 and pp == 3 else None)
            mm_g = p.c("s_mm")
            if uix > 0:
                p.wait("tensor", "s_p1", uix)
            for kc in range(KC):
                lhs = stripe_w[:, buf * 2048 + kc * 256:
                               buf * 2048 + kc * 256 + 128]
                for hf in range(2):
                    pq = pB if hf == 0 else pC
                    for pp in range(2):
                        p.op("tensor", mm(
                            pq[:, pp * 512:(pp + 1) * 512], lhs,
                            xn_fm[:, kc * T_CORE + hf * 1024 + pp * 512:
                                  kc * T_CORE + hf * 1024 + (pp + 1) * 512],
                            kc == 0, kc == KC - 1),
                            inc="s_mm" if kc == KC - 1 and hf == 1 and pp == 1
                            else None)
            mm_y = p.c("s_mm")
            p.wait("scalar", "s_mm", mm_g)
            if uix > 0:
                p.wait("scalar", "s_p1", uix)
            p.op("scalar", lambda e: e.activation(sg_scr, pA[:, 0:T_CORE],
                                                  AF.Silu), inc="s_sg")
            a0 = (uix % 2) * T_CORE
            p.wait("vector", "s_mm", mm_y)
            p.wait("vector", "s_sg", uix + 1)
            if uix >= 2:
                p.wait("vector", "s_st", st_xn_base + 16 * (uix - 1))
            if kind == "e":
                p.op("vector", lambda e, a0=a0: e.tensor_tensor(
                    act_st[:, a0:a0 + 1024], pB[:], sg_scr[:, 0:1024],
                    ALU.mult))
                p.op("vector", lambda e, a0=a0: e.tensor_tensor(
                    act_st[:, a0 + 1024:a0 + 2048], pC[:],
                    sg_scr[:, 1024:2048], ALU.mult), inc="s_p1")
                p.op("vector", lambda e, a0=a0: e.tensor_tensor(
                    act_st[:, a0:a0 + T_CORE], act_st[:, a0:a0 + T_CORE],
                    wb_bf, ALU.mult), inc="s_actst")
            else:
                p.op("vector", lambda e, a0=a0: e.tensor_tensor(
                    act_st[:, a0:a0 + 1024], pB[:], sg_scr[:, 0:1024],
                    ALU.mult))
                p.op("vector", lambda e, a0=a0: e.tensor_tensor(
                    act_st[:, a0 + 1024:a0 + 2048], pC[:],
                    sg_scr[:, 1024:2048], ALU.mult), inc="s_p1")
                p.op("vector", lambda e, a0=a0: e.tensor_copy(
                    act_st[0:1, a0:a0 + 1], act_st[0:1, a0:a0 + 1]),
                    inc="s_actst")
            if use_hw_w2:
                p.wait("vector", "s_w2f", 16 * (uix + 1))
                p.op("vector", lambda e, u=uix: e.tensor_copy(
                    w2slots[u], w2stage[:, 0:1024]), inc="s_w2l")
            p.wait("sync", "s_actst", p.c("s_actst"))
            ld("sync", act_h.ap()[uix, :, :], act_st[:, a0:a0 + T_CORE],
               "s_st")
            if use_hw_w2 and uix + 1 < N_UNITS:
                p.wait("sync", "s_w2l", uix + 1)
                ld("sync", w2stage[:, 0:1024], w2src(uix + 1), "s_w2f")
        p1_all = p.c("s_p1")
        sg_all = p.c("s_sg")
        st_act = p.c("s_st")
        if phases <= 7:
            return finalize()

        A.free("x_fm", "stripe_w", "wb_bf", "act_st", "wrow",
               "sg_scr", "esel_sb")
        if use_hw_w2:
            A.free("w2stage")
        for u in range(n_w2_early, N_UNITS):
            w2slots[u] = A.alloc(f"w2_{u}", 1024)
        xn4b = A.alloc("xn4b", 2 * D, F32)
        act4b = A.alloc("act4b", 2 * 15 * 256)
        res4 = A.alloc("res4", 2 * D, F32)
        scr_b4 = A.alloc("scr_b4", D, F32)

        if n_w2_early < N_UNITS:
            p.wait("gpsimd", "s_p1", p1_all)
            p.wait("gpsimd", "s_st", st_act)
        for u in range(n_w2_early, N_UNITS):
            ld("gpsimd", w2slots[u], w2src(u), "s_w2l")
        w2l_all = p.c("s_w2l")

        NBLK, NGRP = 8, 4
        act0 = p.c("s_act")
        if phases <= 7.5:
            p.wait("sync", "s_p1", p1_all)
            p.wait("sync", "s_st", st_act)
            for gi in range(2):
                for part in range(3):
                    srcx = act_h.ap()[gi * 15 + part * 5:gi * 15 + part * 5 + 5,
                                      :, 0:256]
                    d0 = (gi % 2) * 3840 + part * 5 * 256
                    p.op("sync", lambda e, s=srcx, d0=d0: e.dma_start(
                        out=act4b[:, d0:d0 + 1280].rearrange(
                            "p (j c) -> p j c", j=5),
                        in_=s.transpose([1, 0, 2])), inc="s_act", inc_n=16)
            ld("sync", xn4b[:, 0:D], xn32_h.ap()[0:128, :], "s_xnl")
            return finalize()

        def out_store(i):
            p.wait("sync", "s_vout", i + 1)
            ld("sync", out_t[i * 128:(i + 1) * 128, :],
               res4[:, (i % 2) * D:(i % 2 + 1) * D], "s_out")

        p.wait("sync", "s_p1", p1_all)
        p.wait("sync", "s_st", st_act)
        if use_hw_w2:
            p.wait("sync", "s_w2l", w2l_all)
        for b in range(NBLK):
            for g in range(NGRP):
                gi = b * NGRP + g
                if gi >= 2:
                    p.wait("sync", "s_zdone", gi - 1)
                u0 = g * 15
                for part in range(3):
                    src = act_h.ap()[u0 + part * 5:u0 + part * 5 + 5, :,
                                     b * 256:(b + 1) * 256]
                    d0 = (gi % 2) * 3840 + part * 5 * 256
                    dst = act4b[:, d0:d0 + 5 * 256]
                    p.op("sync", lambda e, s=src, d=dst: e.dma_start(
                        out=d.rearrange("p (j c) -> p j c", j=5),
                        in_=s.transpose([1, 0, 2])), inc="s_act", inc_n=16)
            for tt in range(2):
                i = b * 2 + tt
                if i >= 2:
                    p.wait("sync", "s_vout", i - 1)
                ld("sync", xn4b[:, tt * D:(tt + 1) * D],
                   xn32_h.ap()[i * 128:(i + 1) * 128, :], "s_xnl")
            if b >= 1:
                out_store(2 * b - 2)
                out_store(2 * b - 1)

        p.wait("tensor", "s_w2l", w2l_all)
        p.wait("tensor", "s_p1", p1_all)
        p.wait("tensor", "s_sg", sg_all)
        mm_4b0 = p.c("s_mm")
        for b in range(NBLK):
            if b > 0:
                p.wait("tensor", "s_res", 16 + 2 * b - 1)
                p.wait("tensor", "s_res", 16 + 2 * b)
            for g in range(NGRP):
                gi = b * NGRP + g
                p.wait("tensor", "s_act", act0 + 48 * (gi + 1))
                for tt in range(2):
                    pz = pB if tt == 0 else pC
                    for j in range(15):
                        u = g * 15 + j
                        lhs = act4b[:, (gi % 2) * 3840 + j * 256 + tt * 128:
                                    (gi % 2) * 3840 + j * 256 + (tt + 1) * 128]
                        for hf in range(2):
                            last = j == 14 and hf == 1
                            inc = None
                            if last and tt == 1:
                                inc = "s_zdone"
                            elif last and tt == 0 and g == 3:
                                inc = "s_mm"
                            p.op("tensor", mm(
                                pz[:, hf * 512:(hf + 1) * 512], lhs,
                                w2slots[u][:, hf * 512:(hf + 1) * 512],
                                g == 0 and j == 0, u == N_UNITS - 1), inc=inc)

        for b in range(NBLK):
            for tt in range(2):
                i = b * 2 + tt
                pz = pB if tt == 0 else pC
                rs = res4[:, tt * D:(tt + 1) * D]
                if tt == 0:
                    p.wait("vector", "s_mm", mm_4b0 + b + 1)
                else:
                    p.wait("vector", "s_zdone", 4 * (b + 1))
                p.wait("vector", "s_xnl", xnl_all + 16 * (i + 1))
                if i >= 2:
                    p.wait("vector", "s_out", 16 * (i - 1))
                p.op("vector", lambda e, a=rs, tt=tt, c=pz:
                     e.tensor_tensor(a, xn4b[:, tt * D:(tt + 1) * D], c[:],
                                     ALU.add), inc="s_res")
                p.op("vector", lambda e, a=rs: ln_stats(e, a, 8, scr_b4),
                     inc="s_varg")
                p.wait("vector", "s_exp", 48 + i + 1)
                p.op("vector", lambda e, a=rs: e.tensor_scalar(
                    a, a, ln_st[:, 10:11], ln_st[:, 12:13], ALU.subtract,
                    ALU.mult), inc="s_vout")
        for i in range(16):
            p.wait("scalar", "s_varg", 48 + i + 1)
            p.op("scalar", lambda e: e.activation(ln_st[:, 12:13],
                                                  ln_st[:, 9:10], AF.Ln))
            p.op("scalar", lambda e: e.activation(ln_st[:, 12:13],
                                                  ln_st[:, 12:13], AF.Exp,
                                                  scale=-0.5), inc="s_exp")
        out_store(14)
        out_store(15)
        p.wait("sync", "s_out", p.c("s_out"))

        return finalize()
    return nc



def _host_constants():
    if64 = 1.0 / (500000.0 ** (np.arange(0, 64, 2, dtype=np.float64) / 64.0))
    hi = if64.astype(np.float32)
    lo = (if64 - hi.astype(np.float64)).astype(np.float32)
    signcol = np.where((np.arange(128) % 64) < 32, -1.0, 1.0
                       ).astype(np.float32).reshape(128, 1)
    Pm = np.zeros((128, 128), np.float32)
    for pp in range(128):
        base = 64 * (pp // 64)
        Pm[pp, base + ((pp % 64) + 32) % 64] = 1.0
    pperm = np.ascontiguousarray(Pm.T).astype(ml_dtypes.bfloat16)
    sel = np.zeros((16, 8 * 128), np.float32)
    for c in range(8):
        for m in range(128):
            sel[2 * c + (1 if m >= 64 else 0), c * 128 + m] = 1.0
    sel16 = sel.astype(ml_dtypes.bfloat16)
    ident = np.eye(128, dtype=ml_dtypes.bfloat16)
    esel = np.zeros((8, 8 * 128), np.float32)
    for e in range(8):
        esel[e, e * 128:(e + 1) * 128] = 1.0
    esel = esel.astype(ml_dtypes.bfloat16)
    return hi, lo, signcol, pperm, sel16, ident, esel


def make_in_maps(inputs):
    x = np.ascontiguousarray(np.asarray(inputs["L0_attn_x"], np.float32)
                             ).reshape(2 * SEQ, D)
    hi, lo, signcol, pperm, sel16, ident, esel = _host_constants()
    w_qkv = np.ascontiguousarray(np.asarray(inputs["L0_attn_W_qkv"],
                                            np.float32))
    w_out = np.ascontiguousarray(np.asarray(inputs["L0_attn_W_out"],
                                            np.float32))
    gate_wT = np.ascontiguousarray(
        np.asarray(inputs["L0_ffn_gate_w"], np.float32).T)
    W1T = np.ascontiguousarray(
        np.asarray(inputs["L0_ffn_W1"], np.float32).transpose(0, 2, 1))
    w1s = np.empty((NE, NS_E, D, 256), np.float32)
    for m in range(NS_E):
        w1s[:, m, :, 0:128] = W1T[:, :, 128 * m:128 * (m + 1)]
        w1s[:, m, :, 128:256] = W1T[:, :, H_E + 128 * m:H_E + 128 * (m + 1)]
    Ws1T = np.ascontiguousarray(np.asarray(inputs["L0_ffn_Ws1"],
                                           np.float32).T)
    ws1s = np.empty((NS_S, D, 256), np.float32)
    for m in range(NS_S):
        ws1s[m, :, 0:128] = Ws1T[:, 128 * m:128 * (m + 1)]
        ws1s[m, :, 128:256] = Ws1T[:, H_S + 128 * m:H_S + 128 * (m + 1)]
    w2T = np.ascontiguousarray(
        np.asarray(inputs["L0_ffn_W2"], np.float32).transpose(0, 2, 1))
    ws2T = np.ascontiguousarray(np.asarray(inputs["L0_ffn_Ws2"],
                                           np.float32).T)

    shared = dict(w_qkv=w_qkv, w_out=w_out, gate_wT=gate_wT,
                  w1s=w1s.reshape(NE, NS_E, 8, 128, 256),
                  ws1s=ws1s.reshape(NS_S, 8, 128, 256),
                  w2T=w2T, ws2T=ws2T, signcol=signcol,
                  pperm=pperm, sel16=sel16, ident=ident, esel=esel)
    in_maps = []
    for c in range(N_CORES):
        xs = np.ascontiguousarray(x[c * T_CORE:(c + 1) * T_CORE])
        t0 = (c % 4) * T_CORE
        consts = np.zeros((8, T_CORE), np.float32)
        consts[0] = t0 + np.arange(T_CORE, dtype=np.float32)
        consts[1, 0:HD] = np.concatenate([hi, hi])
        consts[1, HD:2 * HD] = np.concatenate([lo, lo])
        consts[1, 128:256] = np.concatenate([hi, hi, hi, hi])
        consts[1, 256:384] = np.concatenate([lo, lo, lo, lo])
        m = dict(x=xs, xT=np.ascontiguousarray(xs.T), consts=consts)
        m.update(shared)
        in_maps.append(m)
    return in_maps


def kernel(**inputs):
    in_maps = make_in_maps(inputs)
    nc = build_program(taps=TAPS)
    res = run_bass_kernel_spmd(nc, in_maps, core_ids=list(range(N_CORES)))
    outs = [r["out"] for r in res.results]
    return np.concatenate(outs, axis=0).reshape(2, SEQ, D)


# version 09 (final)
# speedup vs baseline: 1.0444x; 1.0213x over previous
import contextlib
import math

import numpy as np
import ml_dtypes

import concourse.bass as bass
import concourse.mybir as mybir
from concourse.bass_utils import run_bass_kernel_spmd

F32 = mybir.dt.float32
BF16 = mybir.dt.bfloat16
I32 = mybir.dt.int32
AF = mybir.ActivationFunctionType
ALU = mybir.AluOpType
AX = mybir.AxisListType

N_CORES = 8
T_CORE = 2048
N_TILES = 16
D = 1024
HD = 64
SEQ = 8192
NE = 8
H_E = 768
H_S = 1536
NS_E = 6
NS_S = 12
N_UNITS = NE * NS_E + NS_S
KC = 8

TWO_PI = 2.0 * math.pi
INV_2PI = 1.0 / TWO_PI
C1 = float(np.float32(TWO_PI))
C2 = float(np.float64(TWO_PI) - np.float64(np.float32(TWO_PI)))
PI = math.pi
EPS_LN = 1e-5
DEN_EPS = 1e-6

ARENA_COLS = 98304

TAPS = ()


class Arena:
    def __init__(self, tensor, cols):
        self.t = tensor
        self.free_list = [(0, cols)]
        self.live = {}
        self.hwm = 0

    def alloc(self, name, cols, dtype=BF16, rows=128):
        unit = 1 if dtype == BF16 else 2
        need = (cols * unit + 15) & ~15
        for idx, (off, sz) in enumerate(self.free_list):
            if sz >= need:
                self.free_list[idx] = (off + need, sz - need)
                if self.free_list[idx][1] == 0:
                    del self.free_list[idx]
                self.live[name] = (off, need)
                self.hwm = max(self.hwm, sum(v[1] for v in self.live.values()))
                ap = self.t[0:rows, off:off + need]
                if dtype != BF16:
                    ap = ap.bitcast(dtype)
                return ap
        raise MemoryError(f"arena: no space for {name} ({need}); "
                          f"free={self.free_list}")

    def free(self, *names):
        for name in names:
            off, sz = self.live.pop(name)
            self.free_list.append((off, sz))
        self.free_list.sort()
        merged = []
        for o, s in self.free_list:
            if merged and merged[-1][0] + merged[-1][1] == o:
                merged[-1][1] += s
            else:
                merged.append([o, s])
        self.free_list = [tuple(x) for x in merged]


class P:
    def __init__(self, nc, ctx):
        self.nc = nc
        self.q = {"tensor": [], "vector": [], "scalar": [], "gpsimd": [],
                  "sync": []}
        self.sem = {}
        self.ctx = ctx

    def mksem(self, *names):
        for n in names:
            h = self.ctx.enter_context(self.nc.semaphore(n))
            self.sem[n] = [h, 0]

    def c(self, name):
        return self.sem[name][1]

    def bump(self, name, n):
        self.sem[name][1] += n
        return self.sem[name][0], self.sem[name][1]

    def wait(self, eng, name, val):
        h = self.sem[name][0]
        if val > 0:
            self.q[eng].append(lambda e: e.wait_ge(h, val))

    def op(self, eng, fn, inc=None, inc_n=1, inc2=None):
        if inc is None:
            self.q[eng].append(fn)
        else:
            h, _ = self.bump(inc, inc_n)
            if inc2 is None:
                self.q[eng].append(lambda e: fn(e).then_inc(h, inc_n))
            else:
                h2, _ = self.bump(inc2, 1)
                self.q[eng].append(
                    lambda e: fn(e).then_inc(h, inc_n).then_inc(h2, 1))

    def emit(self, block):
        for name in ("sync", "gpsimd", "tensor", "vector", "scalar"):
            fns = self.q[name]
            getattr(block, name)(lambda e, fns=fns: [f(e) for f in fns])


def build_program(taps=(), phases=9.0):
    nc = bass.Bass("TRN2", target_bir_lowering=False, debug=False)
    ctx = contextlib.ExitStack()

    di = lambda n, s, dt=F32: nc.dram_tensor(n, s, dt, kind="ExternalInput").ap()
    do = lambda n, s, dt=F32: nc.dram_tensor(n, s, dt, kind="ExternalOutput").ap()

    x_in = di("x", [T_CORE, D])
    xT_in = di("xT", [D, T_CORE])
    w_qkv = di("w_qkv", [D, 3 * D])
    w_out_in = di("w_out", [D, D])
    gate_wT = di("gate_wT", [D, NE])
    w1s = di("w1s", [NE, NS_E, 8, 128, 256])
    ws1s = di("ws1s", [NS_S, 8, 128, 256])
    w2T_in = di("w2T", [NE, H_E, D])
    ws2T_in = di("ws2T", [H_S, D])
    consts_in = di("consts", [8, T_CORE])
    signcol = di("signcol", [128, 1])
    pperm = di("pperm", [128, 128], BF16)
    sel16 = di("sel16", [16, 8 * 128], BF16)
    ident = di("ident", [128, 128], BF16)
    esel = di("esel", [8, 8 * 128], BF16)

    out_t = do("out", [T_CORE, D])

    xn32_h = nc.dram_tensor("xn32_h", [T_CORE, D], F32, kind="Internal")
    xnb_h = nc.dram_tensor("xnb_h", [T_CORE, D], BF16, kind="Internal")
    act_h = nc.dram_tensor("act_h", [N_UNITS, 128, T_CORE], BF16,
                           kind="Internal")
    cc_in = nc.dram_tensor("cc_in", [128, 1040], F32, kind="Internal")
    cc_out = nc.dram_tensor("cc_out", [128, 1040], F32, kind="Internal")

    tap_aps = {}
    for t in taps:
        shape, dt = {
            "q_feat": ([128, 8 * T_CORE], BF16), "kv_full": ([128, 1040], F32),
            "attn": ([128, 8 * T_CORE], BF16), "xn": ([T_CORE, D], F32),
            "scores": ([128, 128], F32), "wrow": ([16, T_CORE], BF16),
            "cosk": ([128, N_TILES * HD], F32),
            "sink": ([128, N_TILES * HD], F32),
            "cosq": ([128, T_CORE], F32), "sinq": ([128, T_CORE], F32),
            "den": ([16, T_CORE], F32), "recip": ([16, T_CORE], BF16),
            "acth": ([4, 128, T_CORE], BF16),
            "xn2": ([T_CORE, D], F32),
            "res0": ([1, 128, D], BF16),
        }[t]
        tap_aps[t] = do("tap_" + t, shape, dt)

    with ctx:
        arena_t = ctx.enter_context(
            nc.sbuf_tensor("arena", [128, ARENA_COLS], BF16))
        A = Arena(arena_t, ARENA_COLS)
        p = P(nc, ctx)
        p.mksem("s_in", "s_gx", "s_gw", "s_gw2", "s_str", "s_w2l", "s_w2f",
                "s_xt",
                "s_xnl", "s_act", "s_st", "s_cc", "s_coll", "s_out",
                "s_mm", "s_vv", "s_ss", "s_tab", "s_ev", "s_qf",
                "s_kfree", "s_vfree", "s_kvin", "s_kvmm", "s_varg", "s_exp",
                "s_qbf", "s_qfree", "s_rotfree", "s_bev", "s_vmul",
                "s_res", "s_sg", "s_p1", "s_actst", "s_zdone", "s_vout",
                "s_gatev", "s_wr")

        scr_a = A.alloc("scr_a", T_CORE, F32)
        scr_b = A.alloc("scr_b", T_CORE, F32)
        sg_scr = A.alloc("sg_scr", T_CORE, F32)
        scr_i = sg_scr.bitcast(I32)
        ln_st = A.alloc("ln_st", 16, F32)
        ones_r = A.alloc("ones_r", 128, rows=1)
        ones_c = A.alloc("ones_c", 8)
        ident_sb = A.alloc("ident_sb", 128)
        sel16_sb = A.alloc("sel16_sb", 8 * 128, rows=16)
        x_fm = A.alloc("x_fm", 8 * T_CORE)
        wqkv_sb = A.alloc("wqkv", 8 * 3072)
        q_feat = A.alloc("q_feat", 8 * T_CORE)
        rows_c = A.alloc("rows_c", T_CORE, F32)
        ivf_r = A.alloc("ivf_r", 512, F32, rows=1)
        sign_sb = A.alloc("sign_sb", 8, F32)
        pperm_sb = A.alloc("pperm_sb", 128)
        cos_k = A.alloc("cos_k", N_TILES * HD, F32)
        sink_s = A.alloc("sink_s", N_TILES * HD, F32)
        cos_q = A.alloc("cos_q", T_CORE, F32)
        sinq_s = A.alloc("sinq_s", T_CORE, F32)
        kfeat_t = A.alloc("kfeat", 1024)
        v_bf = A.alloc("v_bf", 1024)
        q_bf = A.alloc("q_bf", 1024)
        kv_sb = A.alloc("kv_sb", 1040, F32)
        kv_full = A.alloc("kv_full", 1040, F32)
        kv_bf = A.alloc("kv_bf", 1024)
        ksum16 = A.alloc("ksum16", 8 * 16)

        trow_r = rows_c[0:1, :]
        ifhi_r = ivf_r[0:1, 0:HD]
        iflo_r = ivf_r[0:1, HD:2 * HD]
        if128h_r = ivf_r[0:1, 128:256]
        if128l_r = ivf_r[0:1, 256:384]

        pA = ctx.enter_context(nc.psum_tensor("pA", [128, 2048], F32))
        pB = ctx.enter_context(nc.psum_tensor("pB", [128, 1024], F32))
        pCb = ctx.enter_context(nc.psum_tensor("pCb", [128, 2048], BF16))
        pC = pCb[:].bitcast(F32)

        def ld(eng, dst, src, sem):
            p.op(eng, lambda e, d=dst, s=src: e.dma_start(out=d, in_=s),
                 inc=sem, inc_n=16)

        def mm(out, lhsT, rhs, st, sp):
            return lambda e: e.matmul(out, lhsT, rhs, start=st, stop=sp,
                                      skip_group_check=True)

        def finalize():
            tap_srcs = {
                "q_feat": lambda: q_feat, "attn": lambda: attn_fm,
                "kv_full": lambda: kv_full, "cosk": lambda: cos_k,
                "sink": lambda: sink_s, "cosq": lambda: cos_q,
                "sinq": lambda: sinq_s, "den": lambda: den_r,
                "recip": lambda: recip_r, "scores": lambda: scores_sb,
                "wrow": lambda: wrow,
            }
            L = locals()
            for s in list(p.sem):
                cnt = p.c(s)
                if cnt and s not in ("s_st", "s_out"):
                    p.wait("sync", s, cnt)
            for t in taps:
                if t == "res0":
                    ld("sync", tap_aps[t][0], dbg_t, "s_st")
                elif t == "acth":
                    for ii, u in enumerate((0, 5, 30, 48)):
                        ld("sync", tap_aps[t][ii], act_h.ap()[u], "s_st")
                elif t == "xn":
                    ld("sync", tap_aps[t], xn32_h.ap(), "s_st")
                elif t in tap_srcs:
                    try:
                        srcv = tap_srcs[t]()
                    except NameError:
                        continue
                    if t == "wrow":
                        ld("sync", tap_aps[t][0:8, :], srcv[0:8, :], "s_st")
                    else:
                        ld("sync", tap_aps[t], srcv, "s_st")
            if p.c("s_st"):
                p.wait("sync", "s_st", p.c("s_st"))
            if p.c("s_out"):
                p.wait("sync", "s_out", p.c("s_out"))
            block = ctx.enter_context(nc.Block())
            p.emit(block)
            return nc

        ld("sync", rows_c[0:1, :], consts_in[0:1, :], "s_in")
        ld("sync", ivf_r[0:1, 0:512], consts_in[1:2, 0:512], "s_in")
        ld("sync", sign_sb[:, 0:1], signcol, "s_in")
        ld("sync", pperm_sb, pperm, "s_in")
        ld("sync", sel16_sb, sel16, "s_in")
        ld("sync", ident_sb, ident, "s_in")
        n_in = p.c("s_in")

        for c in range(8):
            ld("gpsimd", x_fm[:, c * T_CORE:(c + 1) * T_CORE],
               xT_in[c * 128:(c + 1) * 128, :], "s_gx")
        gx_all = p.c("s_gx")
        for c in range(8):
            ld("gpsimd", wqkv_sb[:, c * 3072 + 1024:c * 3072 + 3072],
               w_qkv[c * 128:(c + 1) * 128, 1024:3072], "s_gw")
        gw_kv = p.c("s_gw")
        for c in range(8):
            ld("gpsimd", wqkv_sb[:, c * 3072:c * 3072 + 1024],
               w_qkv[c * 128:(c + 1) * 128, 0:1024], "s_gw")
        gw_all = p.c("s_gw")

        if phases <= 0.2:
            return finalize()

        p.wait("tensor", "s_in", n_in)
        for i in range(N_TILES):
            sl = trow_r[:, i * 128:(i + 1) * 128]
            p.op("tensor", mm(pB[:, i * HD:(i + 1) * HD], sl, ifhi_r,
                              i % 8 == 0, False))
            p.op("tensor", mm(pB[:, i * HD:(i + 1) * HD], sl, iflo_r,
                              False, True),
                 inc="s_mm" if i == N_TILES - 1 else None)
        mm_ktab = p.c("s_mm")
        for q4 in range(4):
            sl = trow_r[:, q4 * 512:(q4 + 1) * 512]
            p.op("tensor", mm(pA[:, q4 * 512:(q4 + 1) * 512], if128h_r, sl,
                              True, False))
            p.op("tensor", mm(pA[:, q4 * 512:(q4 + 1) * 512], if128l_r, sl,
                              False, True), inc="s_mm" if q4 == 3 else None)
        mm_qtab = p.c("s_mm")
        if phases <= 0.4:
            return finalize()

        p.wait("vector", "s_in", n_in)
        p.op("vector", lambda e: e.memset(ones_r, 1.0))
        p.op("vector", lambda e: e.memset(ones_c[:, 0:1], 1.0))
        p.op("vector", lambda e: e.memset(kv_sb, 0.0))
        p.op("vector", lambda e: e.memset(ksum16, 0.0))

        def range_reduce(v, src, n, r_dst, rc_dst):
            kf = scr_a[:, 0:n]
            v.tensor_single_scalar(kf, src, INV_2PI, ALU.mult)
            v.tensor_copy(scr_i[:, 0:n], kf)
            v.tensor_copy(kf, scr_i[:, 0:n])
            v.scalar_tensor_tensor(r_dst, kf, -C1, src, ALU.mult, ALU.add)
            v.scalar_tensor_tensor(r_dst, kf, -C2, r_dst, ALU.mult, ALU.add)
            v.tensor_single_scalar(rc_dst, r_dst, PI / 2, ALU.add)
            v.tensor_single_scalar(kf, rc_dst, PI, ALU.is_gt)
            return v.scalar_tensor_tensor(rc_dst, kf, -TWO_PI, rc_dst,
                                          ALU.mult, ALU.add)

        nk = N_TILES * HD
        p.wait("vector", "s_mm", mm_ktab)
        p.op("vector", lambda e: range_reduce(e, pB[:, 0:nk], nk,
                                              scr_b[:, 0:nk], pB[:, 0:nk]),
             inc="s_vv")
        vv_kt = p.c("s_vv")
        p.wait("vector", "s_mm", mm_qtab)
        p.op("vector", lambda e: range_reduce(e, pA[:, 0:T_CORE], T_CORE,
                                              scr_b[:, 0:T_CORE],
                                              pA[:, 0:T_CORE]), inc="s_vv")
        vv_qt = p.c("s_vv")
        if phases <= 0.6:
            return finalize()

        p.wait("scalar", "s_vv", vv_kt)
        p.op("scalar", lambda e: e.activation(sink_s, scr_b[:, 0:nk], AF.Sin))
        p.op("scalar", lambda e: e.activation(cos_k, pB[:, 0:nk], AF.Sin),
             inc="s_ss")
        ss_kt = p.c("s_ss")
        p.wait("scalar", "s_vv", vv_qt)
        p.op("scalar", lambda e: e.activation(sinq_s, scr_b[:, 0:T_CORE],
                                              AF.Sin))
        p.op("scalar", lambda e: e.activation(cos_q, pA[:, 0:T_CORE], AF.Sin),
             inc="s_ss")
        ss_qt = p.c("s_ss")
        if phases <= 0.8:
            return finalize()

        p.wait("vector", "s_ss", ss_kt)
        sk4 = sink_s.rearrange("p (i a b) -> p i a b", i=N_TILES, a=2)
        p.op("vector", lambda e: e.tensor_single_scalar(
            sk4[:, :, 0, :], sk4[:, :, 0, :], -1.0, ALU.mult), inc="s_tab")
        tab_k = p.c("s_tab")
        p.wait("vector", "s_ss", ss_qt)
        p.op("vector", lambda e: e.tensor_single_scalar(cos_q, cos_q, 0.125,
                                                        ALU.mult))
        p.op("vector", lambda e: e.tensor_scalar(sinq_s, sinq_s,
                                                 sign_sb[:, 0:1], 0.125,
                                                 ALU.mult, ALU.mult),
             inc="s_tab")
        tab_q = p.c("s_tab")
        if phases <= 1:
            return finalize()

        p.wait("tensor", "s_gx", gx_all)
        p.wait("tensor", "s_gw", gw_kv)
        p.wait("tensor", "s_tab", tab_k)
        mm_1a0 = p.c("s_mm")
        for i in range(N_TILES):
            if i > 0:
                p.wait("tensor", "s_kfree", i)
            for kc in range(KC):
                lhs = x_fm[:, kc * T_CORE + i * 128:kc * T_CORE + (i + 1) * 128]
                for hf in range(2):
                    p.op("tensor", mm(pB[:, hf * 512:(hf + 1) * 512], lhs,
                                      wqkv_sb[:, kc * 3072 + 1024 + hf * 512:
                                              kc * 3072 + 1024 + (hf + 1) * 512],
                                      kc == 0, kc == KC - 1),
                         inc="s_mm" if kc == KC - 1 and hf == 1 else None)
            if i > 0:
                p.wait("tensor", "s_vfree", i)
            for kc in range(KC):
                lhs = x_fm[:, kc * T_CORE + i * 128:kc * T_CORE + (i + 1) * 128]
                for hf in range(2):
                    p.op("tensor", mm(pC[:, hf * 512:(hf + 1) * 512], lhs,
                                      wqkv_sb[:, kc * 3072 + 2048 + hf * 512:
                                              kc * 3072 + 2048 + (hf + 1) * 512],
                                      kc == 0, kc == KC - 1),
                         inc="s_mm" if kc == KC - 1 and hf == 1 else None)
            p.wait("tensor", "s_kvin", i + 1)
            for c in range(8):
                lhs = kfeat_t[:, c * 128:(c + 1) * 128]
                p.op("tensor", mm(pA[:, c * 128:(c + 1) * 128], lhs,
                                  v_bf[:, c * 128:(c + 1) * 128],
                                  i == 0 and c in (0, 4), i == N_TILES - 1))
                p.op("tensor", mm(pA[:, 1024 + c * 16:1024 + c * 16 + 1], lhs,
                                  ones_c[:, 0:1], i == 0 and c == 0,
                                  i == N_TILES - 1),
                     inc=("s_kvmm" if c == 7 and i < N_TILES - 1 else
                          "s_mm" if c == 7 and i == N_TILES - 1 else None))
        mm_kv_done = p.c("s_mm")

        kp4 = pB[:].rearrange("p (h a b) -> p h a b", h=16, a=2)
        kp3 = pB[:].rearrange("p (h d) -> p h d", h=16)
        for i in range(N_TILES):
            cs_b = cos_k[:, i * HD:(i + 1) * HD].unsqueeze(1).broadcast_to(
                [128, 16, HD])
            sn = sink_s[:, i * HD:(i + 1) * HD].rearrange(
                "p (a b) -> p a b", a=2)
            sn0 = sn[:, 0, :].unsqueeze(1).broadcast_to([128, 16, 32])
            sn1 = sn[:, 1, :].unsqueeze(1).broadcast_to([128, 16, 32])
            tc = scr_a[:, 0:1024]
            ts = scr_b[:, 0:1024]
            tc3 = tc.rearrange("p (h d) -> p h d", h=16)
            ts4 = ts.rearrange("p (h a b) -> p h a b", h=16, a=2)
            p.wait("vector", "s_mm", mm_1a0 + 2 * i + 1)
            p.op("vector", lambda e, a=tc3, c=cs_b:
                 e.tensor_tensor(a, kp3, c, ALU.mult))
            p.op("vector", lambda e, a=ts4, c=sn0:
                 e.tensor_tensor(a[:, :, 0, :], kp4[:, :, 1, :], c, ALU.mult))
            p.op("vector", lambda e, a=ts4, c=sn1:
                 e.tensor_tensor(a[:, :, 1, :], kp4[:, :, 0, :], c, ALU.mult),
                 inc="s_kfree")
            p.op("vector", lambda e, a=tc, b=ts:
                 e.tensor_tensor(a, a, b, ALU.add))
            p.op("vector", lambda e, a=ts, b=tc:
                 e.tensor_single_scalar(a, b, 0.0, ALU.min), inc="s_varg")
            p.wait("vector", "s_mm", mm_1a0 + 2 * i + 2)
            if i > 0:
                p.wait("vector", "s_kvmm", i)
            p.op("vector", lambda e: e.tensor_copy(v_bf, pC[:]),
                 inc="s_vfree")
            p.wait("vector", "s_exp", i + 1)
            p.op("vector", lambda e, a=tc: e.scalar_tensor_tensor(
                kfeat_t, a, 0.0, sg_scr[:, 0:1024], ALU.max, ALU.add),
                inc="s_kvin")
        for i in range(N_TILES):
            p.wait("scalar", "s_varg", i + 1)
            p.op("scalar", lambda e: e.activation(sg_scr[:, 0:1024],
                                                  scr_b[:, 0:1024], AF.Exp),
                 inc="s_exp")

        p.wait("vector", "s_mm", mm_kv_done)

        def kv_evac(e):
            for c in range(8):
                e.tensor_copy(kv_sb[0:64, c * 128:c * 128 + 64],
                              pA[0:64, c * 128:c * 128 + 64])
                e.tensor_copy(kv_sb[64:128, c * 128 + 64:c * 128 + 128],
                              pA[64:128, c * 128 + 64:c * 128 + 128])
                e.tensor_copy(kv_sb[:, 1024 + 2 * c:1024 + 2 * c + 1],
                              pA[:, 1024 + c * 16:1024 + c * 16 + 1])
            return e.tensor_copy(kv_sb[0:1, 0:1], kv_sb[0:1, 0:1])
        p.op("vector", kv_evac, inc="s_ev")
        ev_kv = p.c("s_ev")

        p.wait("sync", "s_ev", ev_kv)
        ld("sync", cc_in.ap(), kv_sb, "s_cc")
        cc1 = p.c("s_cc")
        p.wait("gpsimd", "s_cc", cc1)
        p.op("gpsimd", lambda e: e.collective_compute(
            "AllReduce", ALU.add, replica_groups=[[0, 1, 2, 3], [4, 5, 6, 7]],
            ins=[cc_in.ap()], outs=[cc_out.ap()]), inc="s_coll")
        p.wait("sync", "s_coll", 1)
        ld("sync", kv_full, cc_out.ap(), "s_cc")
        cc2 = p.c("s_cc")

        def ksum_build(e):
            for c in range(8):
                e.tensor_copy(ksum16[0:64, c * 16 + 2 * c:c * 16 + 2 * c + 1],
                              kv_full[0:64, 1024 + 2 * c:1024 + 2 * c + 1])
                e.tensor_copy(
                    ksum16[64:128, c * 16 + 2 * c + 1:c * 16 + 2 * c + 2],
                    kv_full[64:128, 1024 + 2 * c:1024 + 2 * c + 1])
            return e.tensor_copy(ksum16[0:1, 0:1], ksum16[0:1, 0:1])
        if phases <= 2:
            return finalize()

        p.wait("tensor", "s_gw", gw_all)
        p.wait("tensor", "s_tab", tab_q)
        p.wait("tensor", "s_ev", ev_kv)
        p.wait("tensor", "s_kfree", N_TILES)
        p.wait("tensor", "s_vfree", N_TILES)
        mm_q0 = p.c("s_mm")
        for idx in range(16):
            m, hf = idx // 2, idx % 2
            pq = pB if hf == 0 else pC
            if idx >= 2:
                p.wait("tensor", "s_qfree", idx - 1)
            for kc in range(KC):
                lhs = wqkv_sb[:, kc * 3072 + m * 128:kc * 3072 + (m + 1) * 128]
                for pp in range(2):
                    p.op("tensor", mm(
                        pq[:, pp * 512:(pp + 1) * 512], lhs,
                        x_fm[:, kc * T_CORE + hf * 1024 + pp * 512:
                             kc * T_CORE + hf * 1024 + (pp + 1) * 512],
                        kc == 0, kc == KC - 1),
                        inc="s_mm" if kc == KC - 1 and pp == 1 else None)
            p.wait("tensor", "s_qbf", idx + 1)
            if idx >= 2:
                p.wait("tensor", "s_rotfree", idx - 1)
            pr = pA[:, (idx % 2) * 1024:(idx % 2) * 1024 + 1024]
            for pp in range(2):
                p.op("tensor", mm(pr[:, pp * 512:(pp + 1) * 512], pperm_sb,
                                  q_bf[:, pp * 512:(pp + 1) * 512],
                                  True, True),
                     inc="s_mm" if pp == 1 else None)

        for idx in range(16):
            m, hf = idx // 2, idx % 2
            pq = pB if hf == 0 else pC
            tsl = slice(hf * 1024, (hf + 1) * 1024)
            pr = pA[:, (idx % 2) * 1024:(idx % 2) * 1024 + 1024]
            p.wait("vector", "s_mm", mm_q0 + 2 * idx + 1)
            p.op("vector", lambda e, a=pq: e.tensor_copy(q_bf, a[:]),
                 inc="s_qbf")
            p.op("vector", lambda e, a=pq, t=tsl: e.tensor_tensor(
                scr_a[:, 0:1024], a[:], cos_q[:, t], ALU.mult), inc="s_qfree")
            p.wait("vector", "s_mm", mm_q0 + 2 * idx + 2)
            p.op("vector", lambda e, a=pr, t=tsl: e.tensor_tensor(
                scr_b[:, 0:1024], a, sinq_s[:, t], ALU.mult), inc="s_rotfree")
            p.op("vector", lambda e: e.tensor_tensor(
                scr_a[:, 0:1024], scr_a[:, 0:1024], scr_b[:, 0:1024],
                ALU.add))
            p.op("vector", lambda e: e.tensor_single_scalar(
                scr_b[:, 0:1024], scr_a[:, 0:1024], 0.0, ALU.min),
                inc="s_varg")
            p.wait("vector", "s_exp", 16 + idx + 1)
            p.op("vector", lambda e, m=m, hf=hf: e.scalar_tensor_tensor(
                q_feat[:, m * T_CORE + hf * 1024:m * T_CORE + (hf + 1) * 1024],
                scr_a[:, 0:1024], 0.0, sg_scr[:, 1024:2048], ALU.max,
                ALU.add), inc="s_qf")
        for idx in range(16):
            p.wait("scalar", "s_varg", 16 + idx + 1)
            p.op("scalar", lambda e: e.activation(sg_scr[:, 1024:2048],
                                                  scr_b[:, 0:1024], AF.Exp),
                 inc="s_exp")
        p.wait("vector", "s_cc", cc2)
        p.op("vector", lambda e: e.tensor_copy(kv_bf, kv_full[:, 0:1024]))
        p.op("vector", ksum_build, inc="s_ev")
        ev_kvbf = p.c("s_ev")
        qf_all = p.c("s_qf")
        mm_1b_done = p.c("s_mm")
        if phases <= 3:
            return finalize()

        A.free("wqkv", "cos_k", "sink_s", "rows_c", "sign_sb", "ivf_r")
        attn_fm = A.alloc("attn_fm", 8 * T_CORE)
        wout_sb = A.alloc("wout_sb", 8 * 1024)
        gate_sb = A.alloc("gate_sb", 64)
        den_r = A.alloc("den_r", T_CORE, F32, rows=16)
        recip_r = A.alloc("recip_r", T_CORE, rows=16)
        p.wait("gpsimd", "s_mm", mm_1b_done)
        for c in range(8):
            ld("gpsimd", wout_sb[:, c * 1024:(c + 1) * 1024],
               w_out_in[c * 128:(c + 1) * 128, :], "s_gw2")
        for c in range(8):
            ld("gpsimd", gate_sb[:, c * 8:(c + 1) * 8],
               gate_wT[c * 128:(c + 1) * 128, :], "s_gw2")
        gw2_all = p.c("s_gw2")

        p.wait("tensor", "s_ev", ev_kvbf)
        p.wait("tensor", "s_qf", qf_all)
        p.wait("tensor", "s_rotfree", 16)
        for pp in range(4):
            for c in range(8):
                p.op("tensor", mm(
                    pA[0:16, pp * 512:(pp + 1) * 512],
                    ksum16[:, c * 16:(c + 1) * 16],
                    q_feat[:, c * T_CORE + pp * 512:c * T_CORE + (pp + 1) * 512],
                    c == 0, c == 7),
                    inc="s_mm" if pp == 3 and c == 7 else None)
        mm_den = p.c("s_mm")
        p.wait("vector", "s_mm", mm_den)
        p.op("vector", lambda e: e.tensor_single_scalar(
            den_r, pA[0:16, 0:T_CORE], DEN_EPS, ALU.max), inc="s_vv")
        vv_den = p.c("s_vv")
        p.wait("scalar", "s_vv", vv_den)
        p.op("scalar", lambda e: e.activation(den_r, den_r, AF.Ln))
        p.op("scalar", lambda e: e.activation(recip_r, den_r, AF.Exp,
                                              scale=-1.0), inc="s_ss")
        ss_recip = p.c("s_ss")

        p.wait("tensor", "s_ss", ss_recip)
        mm_p2 = p.c("s_mm")
        bev_p2 = p.c("s_bev")
        vmul_p2 = p.c("s_vmul")
        for c in range(8):
            for pp in range(4):
                idx = c * 4 + pp
                if idx >= 4:
                    p.wait("tensor", "s_vmul", vmul_p2 + idx - 3)
                p.op("tensor", mm(
                    pA[:, pp * 512:(pp + 1) * 512],
                    kv_bf[:, c * 128:(c + 1) * 128],
                    q_feat[:, c * T_CORE + pp * 512:c * T_CORE + (pp + 1) * 512],
                    True, True))
                if idx >= 2:
                    p.wait("tensor", "s_bev", bev_p2 + idx - 1)
                p.op("tensor", mm(
                    pB[:, (pp % 2) * 512:(pp % 2) * 512 + 512],
                    sel16_sb[:, c * 128:(c + 1) * 128],
                    recip_r[:, pp * 512:(pp + 1) * 512], True, True),
                    inc="s_mm")
        wb2 = scr_b
        for c in range(8):
            for pp in range(4):
                idx = c * 4 + pp
                sl = slice((idx % 2) * 512, (idx % 2) * 512 + 512)
                p.wait("scalar", "s_mm", mm_p2 + idx + 1)
                p.op("scalar", lambda e, s=sl, pp=pp:
                     e.copy(wb2[:, s], pB[:, (pp % 2) * 512:
                                          (pp % 2) * 512 + 512]),
                     inc="s_bev")
                p.wait("vector", "s_bev", bev_p2 + idx + 1)
                p.op("vector", lambda e, c=c, pp=pp, s=sl: e.tensor_tensor(
                    attn_fm[:, c * T_CORE + pp * 512:
                            c * T_CORE + (pp + 1) * 512],
                    pA[:, pp * 512:(pp + 1) * 512], wb2[:, s], ALU.mult),
                    inc="s_vmul")
        vmul_all = p.c("s_vmul")
        if phases <= 4:
            return finalize()

        A.free("q_bf", "kv_sb", "kv_full", "cos_q", "sinq_s", "pperm_sb")
        dbg_t = A.alloc("dbg_t", D, BF16) if "res0" in taps else None
        x_tile = A.alloc("x_tile", 2 * D, F32)
        res_t = A.alloc("res_t", 2 * D, F32)
        xnb_t = A.alloc("xnb_t", 2 * D)

        p.wait("tensor", "s_vmul", vmul_all)
        p.wait("tensor", "s_bev", bev_p2 + 32)
        mm_p3 = p.c("s_mm")
        for i in range(N_TILES):
            pz = pB if i % 2 == 0 else pC
            if i >= 2:
                p.wait("tensor", "s_res", i - 1)
            for kc in range(KC):
                lhs = attn_fm[:, kc * T_CORE + i * 128:
                              kc * T_CORE + (i + 1) * 128]
                for hf in range(2):
                    p.op("tensor", mm(pz[:, hf * 512:(hf + 1) * 512], lhs,
                                      wout_sb[:, kc * 1024 + hf * 512:
                                              kc * 1024 + (hf + 1) * 512],
                                      kc == 0, kc == KC - 1),
                         inc="s_mm" if kc == KC - 1 and hf == 1 else None)

        import os as _os
        _NO_ACC = _os.environ.get("K_NO_ACCUM") == "1"

        def ln_stats(e, src, o, scr):
            e.reduce_sum(ln_st[:, o:o + 1], src, AX.X)
            if _NO_ACC:
                e.tensor_tensor(scr[:, 0:D], src, src, ALU.mult)
                e.reduce_sum(ln_st[:, o + 1:o + 2], scr[:, 0:D], AX.X)
            else:
                e.scalar_tensor_tensor(scr[:, 0:D], src, 1.0, src, ALU.mult,
                                       ALU.mult,
                                       accum_out=ln_st[:, o + 1:o + 2])
            e.tensor_single_scalar(ln_st[:, o + 2:o + 3],
                                   ln_st[:, o:o + 1], 1.0 / D, ALU.mult)
            e.tensor_single_scalar(ln_st[:, o + 1:o + 2],
                                   ln_st[:, o + 1:o + 2], 1.0 / D, ALU.mult)
            e.tensor_tensor(ln_st[:, o + 3:o + 4], ln_st[:, o + 2:o + 3],
                            ln_st[:, o + 2:o + 3], ALU.mult)
            e.tensor_tensor(ln_st[:, o + 1:o + 2], ln_st[:, o + 1:o + 2],
                            ln_st[:, o + 3:o + 4], ALU.subtract)
            return e.tensor_single_scalar(ln_st[:, o + 1:o + 2],
                                          ln_st[:, o + 1:o + 2], EPS_LN,
                                          ALU.add)

        vv_p3_0 = p.c("s_vv")

        def p3_store(i):
            p.wait("sync", "s_vv", vv_p3_0 + i + 1)
            ld("sync", xn32_h.ap()[i * 128:(i + 1) * 128, :],
               res_t[:, (i % 2) * D:(i % 2 + 1) * D], "s_st")
            ld("sync", xnb_h.ap()[i * 128:(i + 1) * 128, :],
               xnb_t[:, (i % 2) * D:(i % 2 + 1) * D], "s_st")

        st_p3_base = p.c("s_st")
        for i in range(N_TILES):
            if i >= 2:
                p.wait("sync", "s_res", i - 1)
            ld("sync", x_tile[:, (i % 2) * D:(i % 2 + 1) * D],
               x_in[i * 128:(i + 1) * 128, :], "s_xt")
            if i >= 2:
                p3_store(i - 2)
        p3_store(N_TILES - 2)
        p3_store(N_TILES - 1)

        for i in range(N_TILES):
            pz = pB if i % 2 == 0 else pC
            xs = x_tile[:, (i % 2) * D:(i % 2 + 1) * D]
            rs = res_t[:, (i % 2) * D:(i % 2 + 1) * D]
            p.wait("vector", "s_mm", mm_p3 + i + 1)
            p.wait("vector", "s_xt", 16 * (i + 1))
            if i >= 2:
                p.wait("vector", "s_st", 32 * (i - 1))
            p.op("vector", lambda e, a=rs, b=xs, c=pz:
                 e.tensor_tensor(a, b, c[:], ALU.add), inc="s_res")
            if dbg_t is not None and i == 0:
                p.op("vector", lambda e, a=rs: e.tensor_copy(dbg_t, a))
            p.op("vector", lambda e, a=rs: ln_stats(e, a, 0, scr_b),
                 inc="s_varg")
            p.wait("vector", "s_exp", 32 + i + 1)
            p.op("vector", lambda e, a=rs: e.tensor_scalar(
                a, a, ln_st[:, 2:3], ln_st[:, 4:5], ALU.subtract, ALU.mult))
            p.op("vector", lambda e, a=rs, i=i: e.tensor_copy(
                xnb_t[:, (i % 2) * D:(i % 2 + 1) * D], a), inc="s_vv")
        for i in range(N_TILES):
            p.wait("scalar", "s_varg", 32 + i + 1)
            p.op("scalar", lambda e: e.activation(ln_st[:, 4:5],
                                                  ln_st[:, 1:2], AF.Ln))
            p.op("scalar", lambda e: e.activation(ln_st[:, 4:5],
                                                  ln_st[:, 4:5], AF.Exp,
                                                  scale=-0.5), inc="s_exp")
        st_xn = p.c("s_st")
        if "xn2" in taps:
            p.wait("sync", "s_st", st_xn)
            ld("sync", tap_aps["xn2"], xn32_h.ap(), "s_st")
            st_xn = p.c("s_st")
        if phases <= 4.9:
            return finalize()

        p.wait("scalar", "s_st", st_xn)
        for c in range(8):
            p.op("scalar", lambda e, c=c: e.dma_start_transpose(
                x_fm[:, c * T_CORE:(c + 1) * T_CORE],
                xnb_h.ap()[:, c * 128:(c + 1) * 128]), inc="s_xnl", inc_n=16)
        xnl_all = p.c("s_xnl")
        xn_fm = x_fm
        if phases <= 5:
            return finalize()

        A.free("den_r", "recip_r", "kfeat", "v_bf")
        scores_sb = A.alloc("scores", 128, F32)
        esel_sb = A.alloc("esel_sb", 8 * 128, rows=8)
        gst = A.alloc("gst", 64, F32)
        w_tmb = A.alloc("w_tmb", 128)
        wrow = A.alloc("wrow", T_CORE, rows=16)

        ld("sync", esel_sb[0:8, :], esel, "s_in")
        n_in2 = p.c("s_in")
        p.wait("tensor", "s_xnl", xnl_all)
        p.wait("tensor", "s_gw2", gw2_all)
        p.wait("tensor", "s_res", 16)
        for i in range(N_TILES):
            for kc in range(KC):
                lhs = xn_fm[:, kc * T_CORE + i * 128:kc * T_CORE + (i + 1) * 128]
                p.op("tensor", mm(pA[:, i * 8:(i + 1) * 8], lhs,
                                  gate_sb[:, kc * 8:(kc + 1) * 8],
                                  kc == 0, kc == KC - 1),
                     inc="s_mm" if i == N_TILES - 1 and kc == KC - 1 else None)
        mm_gate = p.c("s_mm")
        p.wait("vector", "s_mm", mm_gate)

        def gate_v1(e):
            s3 = scores_sb.rearrange("p (i g) -> p i g", g=8)
            e.tensor_copy(scores_sb, pA[:, 0:128])
            e.tensor_reduce(gst[:, 0:16], s3, AX.X, ALU.max)
            g16 = gst[:, 0:16].unsqueeze(2).broadcast_to([128, 16, 8])
            return e.tensor_tensor(s3, s3, g16, ALU.subtract)
        p.op("vector", gate_v1, inc="s_vv")
        vv_g1 = p.c("s_vv")
        p.wait("scalar", "s_vv", vv_g1)
        p.op("scalar", lambda e: e.activation(scores_sb, scores_sb, AF.Exp),
             inc="s_ss")
        ss_g = p.c("s_ss")
        p.wait("vector", "s_ss", ss_g)

        def gate_v2(e):
            s3 = scores_sb.rearrange("p (i g) -> p i g", g=8)
            a3 = scr_a[:, 0:128].rearrange("p (i g) -> p i g", g=8)
            b3 = scr_b[:, 0:128].rearrange("p (i g) -> p i g", g=8)
            e.tensor_reduce(gst[:, 0:16], s3, AX.X, ALU.add)
            e.tensor_copy(gst[:, 48:64], gst[:, 0:16])
            e.reciprocal(gst[:, 16:32], gst[:, 0:16])
            e.tensor_copy(gst[:, 48:64], gst[:, 0:16])
            g16 = gst[:, 16:32].unsqueeze(2).broadcast_to([128, 16, 8])
            e.tensor_tensor(s3, s3, g16, ALU.mult)
            e.tensor_reduce(gst[:, 0:16], s3, AX.X, ALU.max)
            m16 = gst[:, 0:16].unsqueeze(2).broadcast_to([128, 16, 8])
            e.tensor_tensor(a3, s3, m16, ALU.is_ge)
            e.scalar_tensor_tensor(scr_b[:, 0:128], scr_a[:, 0:128], -2.0,
                                   scores_sb, ALU.mult, ALU.add)
            e.tensor_reduce(gst[:, 32:48], b3, AX.X, ALU.max)
            m2 = gst[:, 32:48].unsqueeze(2).broadcast_to([128, 16, 8])
            e.tensor_tensor(b3, s3, m2, ALU.is_ge)
            e.tensor_tensor(scores_sb, scores_sb, scr_b[:, 0:128], ALU.mult)
            return e.tensor_copy(w_tmb, scores_sb)
        p.op("vector", gate_v2, inc="s_gatev")
        p.wait("tensor", "s_gatev", 1)
        for i in range(N_TILES):
            p.op("tensor", lambda e, i=i: e.matmul(
                pCb[0:8, i * 128:(i + 1) * 128],
                w_tmb[:, i * 8:(i + 1) * 8], ident_sb, start=True, stop=True,
                is_transpose=True, skip_group_check=True), inc="s_mm")
        mm_wr = p.c("s_mm")
        p.wait("vector", "s_mm", mm_wr)
        p.op("vector", lambda e: e.tensor_copy(wrow[0:8, :],
                                               pCb[0:8, 0:T_CORE]),
             inc="s_wr")

        if phases <= 6:
            return finalize()

        units = [("e", e_, s_) for e_ in range(NE) for s_ in range(NS_E)] + \
                [("s", 0, s_) for s_ in range(NS_S)]
        A.free("x_tile", "xnb_t", "kv_bf", "ksum16", "scores", "gst",
               "w_tmb", "res_t", "q_feat", "attn_fm", "wout_sb", "gate_sb",
               "sel16_sb", "ident_sb", "scr_a", "scr_b")
        stripe_w = A.alloc("stripe_w", 2 * 8 * 256)
        wb_bf = A.alloc("wb_bf", 2 * T_CORE)
        act_st = A.alloc("act_st", 2 * T_CORE)
        w2slots = {}
        n_w2_early = 48
        for u in range(48):
            w2slots[u] = A.alloc(f"w2_{u}", 1024)
        try:
            for u in range(48, N_UNITS):
                w2slots[u] = A.alloc(f"w2_{u}", 1024)
            n_w2_early = N_UNITS
        except MemoryError:
            pass
        def w2src(u):
            if u < NE * NS_E:
                e_, s_ = u // NS_E, u % NS_E
                return w2T_in[e_, s_ * 128:(s_ + 1) * 128, :]
            s_ = u - NE * NS_E
            return ws2T_in[s_ * 128:(s_ + 1) * 128, :]

        p.wait("gpsimd", "s_wr", 1)
        p.wait("gpsimd", "s_st", st_xn)

        def emit_stripe_load(uix):
            kind, e_, s_ = units[uix]
            buf = uix % 2
            if uix >= 2:
                p.wait("gpsimd", "s_p1", uix - 1)
            src = w1s[e_, s_] if kind == "e" else ws1s[s_]
            dst = stripe_w[:, buf * 2048:(buf + 1) * 2048]
            p.op("gpsimd", lambda e, s=src, d=dst: e.dma_start(
                out=d.rearrange("p (j c) -> p j c", j=8),
                in_=s.transpose([1, 0, 2])), inc="s_str", inc_n=16)

        emit_stripe_load(0)
        emit_stripe_load(1)
        n_w2 = 0
        for uix in range(N_UNITS):
            if uix + 2 < N_UNITS:
                emit_stripe_load(uix + 2)
            if n_w2 < n_w2_early:
                ld("gpsimd", w2slots[n_w2], w2src(n_w2), "s_w2l")
                n_w2 += 1
        while n_w2 < n_w2_early:
            ld("gpsimd", w2slots[n_w2], w2src(n_w2), "s_w2l")
            n_w2 += 1

        p.wait("tensor", "s_wr", 1)
        st_xn_base = st_xn
        for uix, (kind, e_, s_) in enumerate(units):
            buf = uix % 2
            is_exp0 = kind == "e" and s_ == 0
            if is_exp0:
                wbuf = wb_bf[:, (e_ % 2) * T_CORE:(e_ % 2 + 1) * T_CORE]
                if uix > 0:
                    p.wait("tensor", "s_sg", uix)
                p.wait("tensor", "s_in", n_in2)
                for pp in range(4):
                    p.op("tensor", mm(pA[:, pp * 512:(pp + 1) * 512],
                                      esel_sb[0:8, e_ * 128:(e_ + 1) * 128],
                                      wrow[0:8, pp * 512:(pp + 1) * 512],
                                      True, True),
                         inc="s_mm" if pp == 3 else None)
                mmv = p.c("s_mm")
                p.wait("scalar", "s_mm", mmv)
                if e_ >= 2:
                    p.wait("scalar", "s_actst", 6 * e_ - 6)
                bev0 = p.c("s_bev")
                for q in range(4):
                    p.op("scalar", lambda e, q=q, w=wbuf: e.copy(
                        w[:, q * 512:(q + 1) * 512],
                        pA[:, q * 512:(q + 1) * 512]), inc="s_bev")
            p.wait("tensor", "s_str", (uix + 1) * 16)
            if uix > 0 and not is_exp0:
                p.wait("tensor", "s_sg", uix)
            if is_exp0:
                for pp in range(4):
                    p.wait("tensor", "s_bev", bev0 + pp + 1)
                    for kc in range(KC):
                        lhs = stripe_w[:, buf * 2048 + kc * 256 + 128:
                                       buf * 2048 + kc * 256 + 256]
                        p.op("tensor", mm(pA[:, pp * 512:(pp + 1) * 512],
                                          lhs,
                                          xn_fm[:, kc * T_CORE + pp * 512:
                                                kc * T_CORE + (pp + 1) * 512],
                                          kc == 0, kc == KC - 1),
                             inc="s_mm" if kc == KC - 1 and pp == 3 else None)
            else:
                for kc in range(KC):
                    lhs = stripe_w[:, buf * 2048 + kc * 256 + 128:
                                   buf * 2048 + kc * 256 + 256]
                    for pp in range(4):
                        p.op("tensor", mm(pA[:, pp * 512:(pp + 1) * 512],
                                          lhs,
                                          xn_fm[:, kc * T_CORE + pp * 512:
                                                kc * T_CORE + (pp + 1) * 512],
                                          kc == 0, kc == KC - 1),
                             inc="s_mm" if kc == KC - 1 and pp == 3 else None)
            mm_g = p.c("s_mm")
            if uix > 0:
                p.wait("tensor", "s_p1", uix)
            for kc in range(KC):
                lhs = stripe_w[:, buf * 2048 + kc * 256:
                               buf * 2048 + kc * 256 + 128]
                for hf in range(2):
                    pq = pB if hf == 0 else pC
                    for pp in range(2):
                        p.op("tensor", mm(
                            pq[:, pp * 512:(pp + 1) * 512], lhs,
                            xn_fm[:, kc * T_CORE + hf * 1024 + pp * 512:
                                  kc * T_CORE + hf * 1024 + (pp + 1) * 512],
                            kc == 0, kc == KC - 1),
                            inc="s_mm" if kc == KC - 1 and hf == 1 and pp == 1
                            else None)
            mm_y = p.c("s_mm")
            p.wait("scalar", "s_mm", mm_g)
            if uix > 0:
                p.wait("scalar", "s_p1", uix)
            p.op("scalar", lambda e: e.activation(sg_scr, pA[:, 0:T_CORE],
                                                  AF.Silu), inc="s_sg")
            a0 = (uix % 2) * T_CORE
            p.wait("vector", "s_mm", mm_y)
            p.wait("vector", "s_sg", uix + 1)
            if uix >= 2:
                p.wait("vector", "s_st", st_xn_base + 16 * (uix - 1))
            if kind == "e":
                wb_u = wb_bf[:, (e_ % 2) * T_CORE:(e_ % 2 + 1) * T_CORE]
                p.op("vector", lambda e, a0=a0: e.tensor_tensor(
                    act_st[:, a0:a0 + 1024], pB[:], sg_scr[:, 0:1024],
                    ALU.mult))
                p.op("vector", lambda e, a0=a0: e.tensor_tensor(
                    act_st[:, a0 + 1024:a0 + 2048], pC[:],
                    sg_scr[:, 1024:2048], ALU.mult), inc="s_p1")
                p.op("vector", lambda e, a0=a0, w=wb_u: e.tensor_tensor(
                    act_st[:, a0:a0 + T_CORE], act_st[:, a0:a0 + T_CORE],
                    w, ALU.mult), inc="s_actst")
            else:
                p.op("vector", lambda e, a0=a0: e.tensor_tensor(
                    act_st[:, a0:a0 + 1024], pB[:], sg_scr[:, 0:1024],
                    ALU.mult))
                p.op("vector", lambda e, a0=a0: e.tensor_tensor(
                    act_st[:, a0 + 1024:a0 + 2048], pC[:],
                    sg_scr[:, 1024:2048], ALU.mult), inc="s_p1")
                p.op("vector", lambda e, a0=a0: e.tensor_copy(
                    act_st[0:1, a0:a0 + 1], act_st[0:1, a0:a0 + 1]),
                    inc="s_actst")
            p.wait("sync", "s_actst", p.c("s_actst"))
            ld("sync", act_h.ap()[uix, :, :], act_st[:, a0:a0 + T_CORE],
               "s_st")
        p1_all = p.c("s_p1")
        sg_all = p.c("s_sg")
        st_act = p.c("s_st")
        if phases <= 7:
            return finalize()

        A.free("x_fm", "stripe_w", "wb_bf", "act_st", "wrow",
               "sg_scr", "esel_sb")
        for u in range(n_w2_early, N_UNITS):
            w2slots[u] = A.alloc(f"w2_{u}", 1024)
        xn4b = A.alloc("xn4b", 2 * D, F32)
        act4b = A.alloc("act4b", 2 * 15 * 256)
        res4 = A.alloc("res4", 2 * D, F32)
        scr_b4 = A.alloc("scr_b4", D, F32)

        if n_w2_early < N_UNITS:
            p.wait("gpsimd", "s_p1", p1_all)
            p.wait("gpsimd", "s_st", st_act)
        for u in range(n_w2_early, N_UNITS):
            ld("gpsimd", w2slots[u], w2src(u), "s_w2l")
        w2l_all = p.c("s_w2l")

        NBLK, NGRP = 8, 4
        act0 = p.c("s_act")
        if phases <= 7.5:
            p.wait("sync", "s_p1", p1_all)
            p.wait("sync", "s_st", st_act)
            for gi in range(2):
                for part in range(3):
                    srcx = act_h.ap()[gi * 15 + part * 5:gi * 15 + part * 5 + 5,
                                      :, 0:256]
                    d0 = (gi % 2) * 3840 + part * 5 * 256
                    p.op("sync", lambda e, s=srcx, d0=d0: e.dma_start(
                        out=act4b[:, d0:d0 + 1280].rearrange(
                            "p (j c) -> p j c", j=5),
                        in_=s.transpose([1, 0, 2])), inc="s_act", inc_n=16)
            ld("sync", xn4b[:, 0:D], xn32_h.ap()[0:128, :], "s_xnl")
            return finalize()

        def out_store(i):
            p.wait("sync", "s_vout", i + 1)
            ld("sync", out_t[i * 128:(i + 1) * 128, :],
               res4[:, (i % 2) * D:(i % 2 + 1) * D], "s_out")

        p.wait("sync", "s_p1", p1_all)
        p.wait("sync", "s_st", st_act)
        for b in range(NBLK):
            for g in range(NGRP):
                gi = b * NGRP + g
                if gi >= 2:
                    p.wait("sync", "s_zdone", gi - 1)
                u0 = g * 15
                for part in range(3):
                    src = act_h.ap()[u0 + part * 5:u0 + part * 5 + 5, :,
                                     b * 256:(b + 1) * 256]
                    d0 = (gi % 2) * 3840 + part * 5 * 256
                    dst = act4b[:, d0:d0 + 5 * 256]
                    p.op("sync", lambda e, s=src, d=dst: e.dma_start(
                        out=d.rearrange("p (j c) -> p j c", j=5),
                        in_=s.transpose([1, 0, 2])), inc="s_act", inc_n=16)
            for tt in range(2):
                i = b * 2 + tt
                if i >= 2:
                    p.wait("sync", "s_vout", i - 1)
                ld("sync", xn4b[:, tt * D:(tt + 1) * D],
                   xn32_h.ap()[i * 128:(i + 1) * 128, :], "s_xnl")
            if b >= 1:
                out_store(2 * b - 2)
                out_store(2 * b - 1)

        p.wait("tensor", "s_w2l", w2l_all)
        p.wait("tensor", "s_p1", p1_all)
        p.wait("tensor", "s_sg", sg_all)
        mm_4b0 = p.c("s_mm")
        for b in range(NBLK):
            if b > 0:
                p.wait("tensor", "s_res", 16 + 2 * b - 1)
                p.wait("tensor", "s_res", 16 + 2 * b)
            for g in range(NGRP):
                gi = b * NGRP + g
                p.wait("tensor", "s_act", act0 + 48 * (gi + 1))
                for tt in range(2):
                    pz = pB if tt == 0 else pC
                    for j in range(15):
                        u = g * 15 + j
                        lhs = act4b[:, (gi % 2) * 3840 + j * 256 + tt * 128:
                                    (gi % 2) * 3840 + j * 256 + (tt + 1) * 128]
                        for hf in range(2):
                            last = j == 14 and hf == 1
                            inc = None
                            if last and tt == 1:
                                inc = "s_zdone"
                            elif last and tt == 0 and g == 3:
                                inc = "s_mm"
                            p.op("tensor", mm(
                                pz[:, hf * 512:(hf + 1) * 512], lhs,
                                w2slots[u][:, hf * 512:(hf + 1) * 512],
                                g == 0 and j == 0, u == N_UNITS - 1), inc=inc)

        for b in range(NBLK):
            for tt in range(2):
                i = b * 2 + tt
                pz = pB if tt == 0 else pC
                rs = res4[:, tt * D:(tt + 1) * D]
                if tt == 0:
                    p.wait("vector", "s_mm", mm_4b0 + b + 1)
                else:
                    p.wait("vector", "s_zdone", 4 * (b + 1))
                p.wait("vector", "s_xnl", xnl_all + 16 * (i + 1))
                if i >= 2:
                    p.wait("vector", "s_out", 16 * (i - 1))
                p.op("vector", lambda e, a=rs, tt=tt, c=pz:
                     e.tensor_tensor(a, xn4b[:, tt * D:(tt + 1) * D], c[:],
                                     ALU.add), inc="s_res")
                p.op("vector", lambda e, a=rs: ln_stats(e, a, 8, scr_b4),
                     inc="s_varg")
                p.wait("vector", "s_exp", 48 + i + 1)
                p.op("vector", lambda e, a=rs: e.tensor_scalar(
                    a, a, ln_st[:, 10:11], ln_st[:, 12:13], ALU.subtract,
                    ALU.mult), inc="s_vout")
        for i in range(16):
            p.wait("scalar", "s_varg", 48 + i + 1)
            p.op("scalar", lambda e: e.activation(ln_st[:, 12:13],
                                                  ln_st[:, 9:10], AF.Ln))
            p.op("scalar", lambda e: e.activation(ln_st[:, 12:13],
                                                  ln_st[:, 12:13], AF.Exp,
                                                  scale=-0.5), inc="s_exp")
        out_store(14)
        out_store(15)
        p.wait("sync", "s_out", p.c("s_out"))

        return finalize()
    return nc



def _host_constants():
    if64 = 1.0 / (500000.0 ** (np.arange(0, 64, 2, dtype=np.float64) / 64.0))
    hi = if64.astype(np.float32)
    lo = (if64 - hi.astype(np.float64)).astype(np.float32)
    signcol = np.where((np.arange(128) % 64) < 32, -1.0, 1.0
                       ).astype(np.float32).reshape(128, 1)
    Pm = np.zeros((128, 128), np.float32)
    for pp in range(128):
        base = 64 * (pp // 64)
        Pm[pp, base + ((pp % 64) + 32) % 64] = 1.0
    pperm = np.ascontiguousarray(Pm.T).astype(ml_dtypes.bfloat16)
    sel = np.zeros((16, 8 * 128), np.float32)
    for c in range(8):
        for m in range(128):
            sel[2 * c + (1 if m >= 64 else 0), c * 128 + m] = 1.0
    sel16 = sel.astype(ml_dtypes.bfloat16)
    ident = np.eye(128, dtype=ml_dtypes.bfloat16)
    esel = np.zeros((8, 8 * 128), np.float32)
    for e in range(8):
        esel[e, e * 128:(e + 1) * 128] = 1.0
    esel = esel.astype(ml_dtypes.bfloat16)
    return hi, lo, signcol, pperm, sel16, ident, esel


def make_in_maps(inputs):
    x = np.ascontiguousarray(np.asarray(inputs["L0_attn_x"], np.float32)
                             ).reshape(2 * SEQ, D)
    hi, lo, signcol, pperm, sel16, ident, esel = _host_constants()
    w_qkv = np.ascontiguousarray(np.asarray(inputs["L0_attn_W_qkv"],
                                            np.float32))
    w_out = np.ascontiguousarray(np.asarray(inputs["L0_attn_W_out"],
                                            np.float32))
    gate_wT = np.ascontiguousarray(
        np.asarray(inputs["L0_ffn_gate_w"], np.float32).T)
    W1T = np.ascontiguousarray(
        np.asarray(inputs["L0_ffn_W1"], np.float32).transpose(0, 2, 1))
    w1s = np.empty((NE, NS_E, D, 256), np.float32)
    for m in range(NS_E):
        w1s[:, m, :, 0:128] = W1T[:, :, 128 * m:128 * (m + 1)]
        w1s[:, m, :, 128:256] = W1T[:, :, H_E + 128 * m:H_E + 128 * (m + 1)]
    Ws1T = np.ascontiguousarray(np.asarray(inputs["L0_ffn_Ws1"],
                                           np.float32).T)
    ws1s = np.empty((NS_S, D, 256), np.float32)
    for m in range(NS_S):
        ws1s[m, :, 0:128] = Ws1T[:, 128 * m:128 * (m + 1)]
        ws1s[m, :, 128:256] = Ws1T[:, H_S + 128 * m:H_S + 128 * (m + 1)]
    w2T = np.ascontiguousarray(
        np.asarray(inputs["L0_ffn_W2"], np.float32).transpose(0, 2, 1))
    ws2T = np.ascontiguousarray(np.asarray(inputs["L0_ffn_Ws2"],
                                           np.float32).T)

    shared = dict(w_qkv=w_qkv, w_out=w_out, gate_wT=gate_wT,
                  w1s=w1s.reshape(NE, NS_E, 8, 128, 256),
                  ws1s=ws1s.reshape(NS_S, 8, 128, 256),
                  w2T=w2T, ws2T=ws2T, signcol=signcol,
                  pperm=pperm, sel16=sel16, ident=ident, esel=esel)
    in_maps = []
    for c in range(N_CORES):
        xs = np.ascontiguousarray(x[c * T_CORE:(c + 1) * T_CORE])
        t0 = (c % 4) * T_CORE
        consts = np.zeros((8, T_CORE), np.float32)
        consts[0] = t0 + np.arange(T_CORE, dtype=np.float32)
        consts[1, 0:HD] = np.concatenate([hi, hi])
        consts[1, HD:2 * HD] = np.concatenate([lo, lo])
        consts[1, 128:256] = np.concatenate([hi, hi, hi, hi])
        consts[1, 256:384] = np.concatenate([lo, lo, lo, lo])
        m = dict(x=xs, xT=np.ascontiguousarray(xs.T), consts=consts)
        m.update(shared)
        in_maps.append(m)
    return in_maps


def kernel(**inputs):
    in_maps = make_in_maps(inputs)
    nc = build_program(taps=TAPS)
    res = run_bass_kernel_spmd(nc, in_maps, core_ids=list(range(N_CORES)))
    outs = [r["out"] for r in res.results]
    return np.concatenate(outs, axis=0).reshape(2, SEQ, D)
